# version 02 (final)
# baseline (speedup 1.0000x reference)
import numpy as np
import ml_dtypes
from contextlib import ExitStack

import concourse.bass as bass
import concourse.mybir as mybir
from concourse.bass_utils import run_bass_kernel_spmd

F32 = mybir.dt.float32
BF16 = mybir.dt.bfloat16
AF = mybir.ActivationFunctionType
ALU = mybir.AluOpType

B, L, D = 2, 8192, 1024
NCORES = 8
TPC = 2048
D_INNER = 2048
DT_RANK = 64
HALO = 128
TCH = TPC + HALO
NT = TCH // 128
MAGIC = 12582912.0

BFD = ml_dtypes.bfloat16


def _mk(ctx, nc, names):
    return {n: ctx.enter_context(nc.semaphore(n)) for n in names}



def build_l0front():
    nc = bass.Bass("TRN2", target_bir_lowering=False)
    xh = nc.dram_tensor("xh", [TCH, D], F32, kind="ExternalInput").ap()
    w_in = nc.dram_tensor("w_in", [D, 2 * D_INNER], F32, kind="ExternalInput").ap()
    n1w = nc.dram_tensor("n1w", [1, D], F32, kind="ExternalInput").ap()
    convw_t = nc.dram_tensor("convw_t", [128, 64], F32, kind="ExternalInput").ap()
    convb_t = nc.dram_tensor("convb_t", [128, 16], F32, kind="ExternalInput").ap()
    xpw = nc.dram_tensor("xpw", [D_INNER, 96], F32, kind="ExternalInput").ap()
    dtw = nc.dram_tensor("dtw", [DT_RANK, D_INNER], F32, kind="ExternalInput").ap()
    dtb_t = nc.dram_tensor("dtb_t", [128, 16], F32, kind="ExternalInput").ap()
    ident = nc.dram_tensor("ident", [128, 128], BF16, kind="ExternalInput").ap()

    o_xc = nc.dram_tensor("o_xc", [D_INNER, TPC], F32, kind="ExternalOutput").ap()
    o_z = nc.dram_tensor("o_z", [D_INNER, TPC], F32, kind="ExternalOutput").ap()
    o_dt = nc.dram_tensor("o_dt", [D_INNER, TPC], F32, kind="ExternalOutput").ap()
    o_BC = nc.dram_tensor("o_BC", [32, TPC], F32, kind="ExternalOutput").ap()

    NOC = 32
    BLKS = [(0, 512), (512, 512), (1024, 512), (1536, 512), (2048, 128)]
    MBLKS = [(0, 512), (512, 512), (1024, 512), (1536, 512)]

    with ExitStack() as ctx:
        sb = lambda name, shape, dt=F32: ctx.enter_context(nc.sbuf_tensor(name, shape, dt))
        identt = sb("identt", [128, 128], BF16)
        n1row = sb("n1row", [1, D])
        n1rep = sb("n1rep", [128, D])
        onesr = sb("onesr", [1, 128])
        convwt = sb("convwt", [128, 64])
        convbt = sb("convbt", [128, 16])
        dtbt = sb("dtbt", [128, 16])
        xpwt = sb("xpwt", [128, 16 * 96], BF16)
        dtwt = sb("dtwt", [64, D_INNER], BF16)
        xt = [sb(f"xt{i}", [128, D]) for i in range(2)]
        sqscr = sb("sqscr", [128, D])
        fla = sb("fla", [128, 256])
        flb = sb("flb", [128, 256])
        rstat = sb("rstat", [128, NT])
        rrec = sb("rrec", [128, NT])
        rtmp = sb("rtmp", [128, NT])
        xnbf = [sb(f"xnbf{i}", [128, D], BF16) for i in range(2)]
        xnT = [sb(f"xnT{k}", [128, TCH], BF16) for k in range(8)]
        wbuf = [sb(f"wbuf{p}", [128, 8 * 128], BF16) for p in range(2)]
        rowt = [sb(f"rowt{p}", [128, TCH]) for p in range(2)]
        convo = [sb(f"convo{p}", [128, TPC]) for p in range(2)]
        xcact = [sb(f"xcact{p}", [128, TPC]) for p in range(2)]
        xcb = [sb(f"xcb{k}", [128, TPC], BF16) for k in range(16)]
        projT = sb("projT", [96, TPC])
        dtrbf = sb("dtrbf", [64, TPC], BF16)
        ascr = sb("ascr", [128, TPC])
        dtout = xcact
        uscr, lscr = convo[1], convo[0]

        ps = [ctx.enter_context(nc.psum_tensor(f"ps{i}", [128, 512], F32))
              for i in range(6)]
        psT = [ctx.enter_context(nc.psum_tensor(f"psT{i}", [128, 128], BF16))
               for i in range(2)]

        S = _mk(ctx, nc, ["LX0", "LX1", "LW", "SQ", "NRM", "BC", "XN", "TP", "TE",
                          "WLD0", "WLD1", "MM", "MEV", "CNV", "SIL", "XCB",
                          "STX0", "STX1", "STZ0", "STZ1", "PRJ", "PEV", "DTR",
                          "DTM", "DTE", "SP2", "DTF", "STD0", "STD1", "STBC"])
        LW_ALL = 16 * 22
        blk = ctx.enter_context(nc.Block())

        rowt_req = [None] * 2

        @blk.sync
        def _(sync):
            for i in range(NT):
                if i >= 2:
                    sync.wait_ge(S["SQ"], i - 1)
                sync.dma_start(out=xt[i % 2][:], in_=xh[i * 128:(i + 1) * 128, :]).then_inc(S[f"LX{i % 2}"], 16)
            sync.wait_ge(S["SQ"], NT)
            for j in range(NT):
                if j >= 2:
                    sync.wait_ge(S["XN"], j - 1)
                sync.dma_start(out=xt[j % 2][:], in_=xh[j * 128:(j + 1) * 128, :]).then_inc(S[f"LX{j % 2}"], 16)
            for oc in range(NOC):
                if oc < 16:
                    sync.wait_ge(S["SIL"], oc + 1)
                    sync.dma_start(out=o_xc[oc * 128:(oc + 1) * 128, :],
                                   in_=xcact[oc % 2][:]).then_inc(S[f"STX{oc % 2}"], 16)
                else:
                    sync.wait_ge(S["MEV"], 5 * (oc + 1))
                    sync.dma_start(out=o_z[(oc - 16) * 128:(oc - 15) * 128, :],
                                   in_=rowt[oc % 2][:, HALO:TCH]).then_inc(S[f"STZ{oc % 2}"], 16)
            sync.wait_ge(S["PEV"], 4)
            sync.dma_start(out=o_BC[:, :], in_=projT[64:96, :]).then_inc(S["STBC"], 16)
            for oc in range(16):
                sync.wait_ge(S["DTF"], oc + 1)
                sync.dma_start(out=o_dt[oc * 128:(oc + 1) * 128, :],
                               in_=dtout[oc % 2][:]).then_inc(S[f"STD{oc % 2}"], 16)

        @blk.gpsimd
        def _(g):
            g.dma_start(out=n1row[:], in_=n1w[:, :]).then_inc(S["LW"], 16)
            g.dma_start(out=identt[:], in_=ident[:, :]).then_inc(S["LW"], 16)
            g.dma_start(out=convwt[:], in_=convw_t[:, :]).then_inc(S["LW"], 16)
            g.dma_start(out=convbt[:], in_=convb_t[:, :]).then_inc(S["LW"], 16)
            g.dma_start(out=dtbt[:], in_=dtb_t[:, :]).then_inc(S["LW"], 16)
            g.dma_start(out=dtwt[:], in_=dtw[:, :]).then_inc(S["LW"], 16)
            for k in range(16):
                g.dma_start(out=xpwt[:, k * 96:(k + 1) * 96],
                            in_=xpw[k * 128:(k + 1) * 128, :]).then_inc(S["LW"], 16)
            for oc in range(NOC):
                if oc >= 2:
                    g.wait_ge(S["MM"], 40 * (oc - 1))
                for k in range(8):
                    g.dma_start(out=wbuf[oc % 2][:, k * 128:(k + 1) * 128],
                                in_=w_in[k * 128:(k + 1) * 128,
                                         oc * 128:(oc + 1) * 128]).then_inc(S[f"WLD{oc % 2}"], 16)
            for nm in ["STX0", "STX1", "STZ0", "STZ1", "STD0", "STD1"]:
                g.wait_ge(S[nm], 16 * 8)
            g.wait_ge(S["STBC"], 16)

        @blk.scalar
        def _(act):
            for i in range(NT):
                act.wait_ge(S[f"LX{i % 2}"], 16 * (i // 2 + 1))
                act.activation(out=sqscr[:], in_=xt[i % 2][:], func=AF.Square,
                               accum_out=rstat[:, i:i + 1]).then_inc(S["SQ"], 1)
            act.wait_ge(S["NRM"], 1)
            act.activation(out=rstat[:], in_=rtmp[:], func=AF.Sqrt).then_inc(S["SQ"], 1)
            act.wait_ge(S["BC"], 2)
            act.copy(n1rep[:, 0:512], ps[0][:])
            act.copy(n1rep[:, 512:1024], ps[1][:]).then_inc(S["BC"], 1)
            for j in range(NT):
                for t in range(8):
                    gidx = j * 8 + t
                    act.wait_ge(S["TP"], gidx + 1)
                    act.copy(xnT[t][:, j * 128:(j + 1) * 128], psT[gidx % 2][:]).then_inc(S["TE"], 1)
            nsil = 0
            for oc in range(NOC):
                req = rowt_req[oc % 2]
                if req is not None:
                    act.wait_ge(S[req[0]], req[1])
                act.wait_ge(S["MM"], 40 * (oc + 1))
                for bi, (c0, cw) in enumerate(BLKS):
                    last = bi == len(BLKS) - 1
                    ins = act.copy(rowt[oc % 2][:, c0:c0 + cw], ps[bi][0:128, 0:cw])
                    if last:
                        ins.then_inc(S["MEV"], 5)
                if oc < 16:
                    rowt_req[oc % 2] = ("CNV", oc + 1)
                else:
                    rowt_req[oc % 2] = (f"STZ{oc % 2}", 16 * ((oc - 16) // 2 + 1))
                if oc < 16:
                    act.wait_ge(S["CNV"], oc + 1)
                    if oc >= 2:
                        act.wait_ge(S[f"STX{oc % 2}"], 16 * (oc // 2))
                        act.wait_ge(S["XCB"], oc - 1)
                    act.activation(out=xcact[oc % 2][:], in_=convo[oc % 2][:],
                                   func=AF.Silu, bias=convbt[:, oc:oc + 1]).then_inc(S["SIL"], 1)
                    nsil += 1
            act.wait_ge(S["PRJ"], 64)
            for bi, (c0, cw) in enumerate(MBLKS):
                act.copy(projT[0:96, c0:c0 + cw], ps[bi][0:96, :]).then_inc(S["PEV"], 1)
            for oc in range(16):
                act.wait_ge(S["DTE"], oc + 1)
                if oc >= 1:
                    act.wait_ge(S["DTF"], oc)
                act.activation(out=ascr[:], in_=uscr[:], func=AF.Abs)
                act.activation(out=lscr[:], in_=ascr[:], func=AF.Exp, scale=-1.0)
                act.activation(out=ascr[:], in_=lscr[:], func=AF.Ln, bias=1.0).then_inc(S["SP2"], 1)

        @blk.vector
        def _(v):
            v.memset(onesr[:], 1.0).then_inc(S["BC"], 1)
            v.wait_ge(S["SQ"], NT)
            v.tensor_scalar(rtmp[:], rstat[:], 1.0 / D, 1e-6,
                            ALU.mult, ALU.add).then_inc(S["NRM"], 1)
            v.wait_ge(S["SQ"], NT + 1)
            v.reciprocal(rrec[:], rstat[:]).then_inc(S["NRM"], 1)
            v.tensor_copy(flb[:], fla[:])
            v.wait_ge(S["BC"], 3)
            p1 = [16 * 9, 16 * 8]
            for j in range(NT):
                v.wait_ge(S[f"LX{j % 2}"], p1[j % 2] + 16 * (j // 2 + 1))
                if j >= 2:
                    v.wait_ge(S["TP"], 8 * (j - 1))
                v.scalar_tensor_tensor(xnbf[j % 2][:], xt[j % 2][:], rrec[:, j:j + 1],
                                       n1rep[:], ALU.mult, ALU.mult).then_inc(S["XN"], 1)
            for oc in range(16):
                v.wait_ge(S["MEV"], 5 * (oc + 1))
                rt = rowt[oc % 2]
                co = convo[oc % 2]
                if oc >= 2:
                    v.wait_ge(S["SIL"], oc - 1)
                v.tensor_scalar_mul(co[:], rt[:, HALO - 3:TCH - 3], convwt[:, oc * 4:oc * 4 + 1])
                v.scalar_tensor_tensor(co[:], rt[:, HALO - 2:TCH - 2], convwt[:, oc * 4 + 1:oc * 4 + 2],
                                       co[:], ALU.mult, ALU.add)
                v.scalar_tensor_tensor(co[:], rt[:, HALO - 1:TCH - 1], convwt[:, oc * 4 + 2:oc * 4 + 3],
                                       co[:], ALU.mult, ALU.add)
                v.scalar_tensor_tensor(co[:], rt[:, HALO:TCH], convwt[:, oc * 4 + 3:oc * 4 + 4],
                                       co[:], ALU.mult, ALU.add).then_inc(S["CNV"], 1)
                v.wait_ge(S["SIL"], oc + 1)
                v.tensor_copy(xcb[oc][:], xcact[oc % 2][:]).then_inc(S["XCB"], 1)
            v.wait_ge(S["PEV"], 4)
            v.tensor_copy(dtrbf[:], projT[0:64, :]).then_inc(S["DTR"], 1)
            for oc in range(16):
                v.wait_ge(S["DTM"], oc + 1)
                for bi, (c0, cw) in enumerate(MBLKS):
                    ins = v.tensor_scalar_add(uscr[:, c0:c0 + cw], ps[bi][0:128, :],
                                              dtbt[:, oc:oc + 1])
                    if bi == 3:
                        ins.then_inc(S["DTE"], 1)
                v.wait_ge(S["SP2"], oc + 1)
                if oc == 0:
                    v.wait_ge(S["STX0"], 16 * 8)
                    v.wait_ge(S["STX1"], 16 * 8)
                    v.wait_ge(S["XCB"], 16)
                if oc >= 2:
                    v.wait_ge(S[f"STD{oc % 2}"], 16 * (oc // 2))
                v.tensor_single_scalar(dtout[oc % 2][:], uscr[:], 0.0, ALU.max)
                v.tensor_tensor(dtout[oc % 2][:], dtout[oc % 2][:], ascr[:],
                                ALU.add).then_inc(S["DTF"], 1)

        @blk.tensor
        def _(te):
            te.wait_ge(S["BC"], 1)
            te.wait_ge(S["LW"], LW_ALL)
            te.matmul(ps[0][:], onesr[:, :], n1row[:, 0:512], start=True, stop=True)
            te.matmul(ps[1][:], onesr[:, :], n1row[:, 512:1024], start=True,
                      stop=True).then_inc(S["BC"], 2)

            for j in range(NT):
                te.wait_ge(S["XN"], j + 1)
                for t in range(8):
                    gidx = j * 8 + t
                    if gidx >= 2:
                        te.wait_ge(S["TE"], gidx - 1)
                    te.transpose(psT[gidx % 2][:], xnbf[j % 2][:, t * 128:(t + 1) * 128],
                                 identt[:]).then_inc(S["TP"], 1)
            for oc in range(NOC):
                te.wait_ge(S[f"WLD{oc % 2}"], 128 * (oc // 2 + 1))
                if oc >= 1:
                    te.wait_ge(S["MEV"], 5 * oc)
                for bi, (c0, cw) in enumerate(BLKS):
                    for k in range(8):
                        ins = te.matmul(ps[bi][0:128, 0:cw],
                                        wbuf[oc % 2][:, k * 128:(k + 1) * 128],
                                        xnT[k][:, c0:c0 + cw],
                                        start=(k == 0), stop=(k == 7))
                        if bi == len(BLKS) - 1 and k == 7:
                            ins.then_inc(S["MM"], 40)
            te.wait_ge(S["XCB"], 16)
            te.wait_ge(S["MEV"], 5 * NOC)
            for bi, (c0, cw) in enumerate(MBLKS):
                for k in range(16):
                    ins = te.matmul(ps[bi][0:96, :], xpwt[:, k * 96:(k + 1) * 96],
                                    xcb[k][:, c0:c0 + cw], start=(k == 0), stop=(k == 15))
                    if bi == 3 and k == 15:
                        ins.then_inc(S["PRJ"], 64)
            te.wait_ge(S["DTR"], 1)
            te.wait_ge(S["PEV"], 4)
            for oc in range(16):
                if oc >= 1:
                    te.wait_ge(S["DTE"], oc)
                for bi, (c0, cw) in enumerate(MBLKS):
                    ins = te.matmul(ps[bi][0:128, :], dtwt[0:64, oc * 128:(oc + 1) * 128],
                                    dtrbf[0:64, c0:c0 + cw], start=True, stop=True)
                    if bi == 3:
                        ins.then_inc(S["DTM"], 1)

    return nc



def build_scan():
    TH = 2048
    NTH = L // TH
    nc = bass.Bass("TRN2", target_bir_lowering=False)
    xc_r = nc.dram_tensor("xc_r", [256, B * L], F32, kind="ExternalInput").ap()
    dt_r = nc.dram_tensor("dt_r", [256, B * L], F32, kind="ExternalInput").ap()
    Bf = nc.dram_tensor("Bf", [B, 16, L], F32, kind="ExternalInput").ap()
    Cf = nc.dram_tensor("Cf", [B, 16, L], F32, kind="ExternalInput").ap()
    alog_r = nc.dram_tensor("alog_r", [256, 16], F32, kind="ExternalInput").ap()
    o_ys = nc.dram_tensor("o_ys", [256, B * L], F32, kind="ExternalOutput").ap()

    NCH = TH // 512

    with ExitStack() as ctx:
        sb = lambda name, shape, dt=F32: ctx.enter_context(nc.sbuf_tensor(name, shape, dt))
        onesr = sb("onesr", [1, 128])
        alogt = sb("alogt", [128, 32])
        acol = sb("acol", [128, 32])
        brow = sb("brow", [1, TH])
        crow = sb("crow", [1, TH])
        dtt = [sb(f"dtt{i}", [128, TH]) for i in range(2)]
        gt = [sb(f"gt{i}", [128, TH]) for i in range(2)]
        brep = sb("brep", [128, TH])
        crep = sb("crep", [128, TH])
        abuf = sb("abuf", [128, TH + 1])
        bsb = sb("bsb", [128, TH + 1])
        hs = [sb(f"hs{i}", [128, TH + 1]) for i in range(2)]
        hct = sb("hct", [128, TH])
        yacc = [sb(f"yacc{i}", [128, TH]) for i in range(2)]
        scst = [sb(f"scst{i}", [128, 16]) for i in range(2)]

        ps = [ctx.enter_context(nc.psum_tensor(f"ps{i}", [128, 512], F32))
              for i in range(8)]

        S = _mk(ctx, nc, ["LP", "LD0", "LD1", "BRL", "PMM", "PEV", "AEX",
                          "BSD", "SC", "GD", "YST", "INIT", "AEXP", "NEG",
                          "SCP", "CIN"])
        blk = ctx.enter_context(nc.Block())

        SEGS = [(b, th) for b in range(B) for th in range(NTH)]

        scp_through = {}
        _cnt = 0
        for _si, (_b, _th) in enumerate(SEGS):
            for _s in range(16):
                for _ct in range(2):
                    if _th < NTH - 1:
                        _cnt += 1
                    scp_through[(_si, _s, _ct)] = _cnt

        def prev_pos(si, s, ct):
            return (si, s - 1, ct) if s > 0 else (si - 1, 15, ct)

        def scp_need(si, b, s, ct):
            return 32 * (si - 1) - 32 * b + 2 * s + ct + 1

        @blk.sync
        def _(sync):
            for si, (b, th) in enumerate(SEGS):
                c0 = b * L + th * TH
                if si >= 1:
                    sync.wait_ge(S["AEX"], 32 * si)
                    sync.wait_ge(S["BSD"], 32 * si)
                for ct in range(2):
                    sync.dma_start(out=dtt[ct][:], in_=dt_r[ct * 128:(ct + 1) * 128,
                                                           c0:c0 + TH]).then_inc(S[f"LD{ct}"], 16)
                    sync.dma_start(out=gt[ct][:], in_=xc_r[ct * 128:(ct + 1) * 128,
                                                           c0:c0 + TH]).then_inc(S[f"LD{ct}"], 16)
                for s in range(16):
                    st_ = si * 16 + s
                    if st_ >= 1:
                        sync.wait_ge(S["PMM"], 2 * st_)
                    sync.dma_start(out=brow[:], in_=Bf[b, s:s + 1, th * TH:(th + 1) * TH]).then_inc(S["BRL"], 16)
                    sync.dma_start(out=crow[:], in_=Cf[b, s:s + 1, th * TH:(th + 1) * TH]).then_inc(S["BRL"], 16)
                sync.wait_ge(S["GD"], 32 * (si + 1))
                for ct in range(2):
                    sync.dma_start(out=o_ys[ct * 128:(ct + 1) * 128, c0:c0 + TH],
                                   in_=yacc[ct][:]).then_inc(S["YST"], 16)

        @blk.gpsimd
        def _(g):
            g.dma_start(out=alogt[:, 0:16], in_=alog_r[0:128, :]).then_inc(S["LP"], 16)
            g.dma_start(out=alogt[:, 16:32], in_=alog_r[128:256, :]).then_inc(S["LP"], 16)
            for si, (b, th) in enumerate(SEGS):
                for s in range(16):
                    st_ = si * 16 + s
                    for ct in range(2):
                        g.wait_ge(S["SC"], 2 * st_ + ct + 1)
                        if s == 0:
                            g.wait_ge(S["YST"], 32 * si)
                            g.tensor_tensor(yacc[ct][:], hs[ct][:, 1:TH + 1], crep[:],
                                            ALU.mult).then_inc(S["GD"], 1)
                        else:
                            g.tensor_tensor(hct[:], hs[ct][:, 1:TH + 1], crep[:], ALU.mult)
                            g.tensor_tensor(yacc[ct][:], yacc[ct][:], hct[:],
                                            ALU.add).then_inc(S["GD"], 1)
                        if th < NTH - 1:
                            g.tensor_copy(scst[ct][:, s:s + 1],
                                          hs[ct][:, TH:TH + 1]).then_inc(S["SCP"], 1)
            g.wait_ge(S["YST"], 16 * 2 * len(SEGS))

        @blk.scalar
        def _(act):
            act.wait_ge(S["LP"], 32)
            act.activation(out=acol[:], in_=alogt[:], func=AF.Exp).then_inc(S["AEXP"], 1)
            act.memzero(abuf[:, 0:1])
            act.wait_ge(S["NEG"], 1)
            rounds = 0
            cin = 0
            for si, (b, th) in enumerate(SEGS):
                for s in range(16):
                    st_ = si * 16 + s
                    if st_ >= 1:
                        act.wait_ge(S["BSD"], 2 * st_)
                        act.wait_ge(S["GD"], 2 * st_)
                    for rep, dest in ((0, brep), (1, crep)):
                        act.wait_ge(S["PMM"], rounds + 1)
                        for i in range(NCH):
                            ins = act.copy(dest[:, i * 512:(i + 1) * 512],
                                           ps[(rounds % 2) * 4 + i][:])
                            if i == NCH - 1:
                                ins.then_inc(S["PEV"], 1)
                        rounds += 1
                    for ct in range(2):
                        if st_ >= 1 or ct == 1:
                            act.wait_ge(S["SC"], 2 * st_ + ct)
                        act.wait_ge(S[f"LD{ct}"], 32 * (si + 1))
                        act.activation(out=abuf[:, 1:TH + 1], in_=dtt[ct][:], func=AF.Exp,
                                       scale=acol[:, ct * 16 + s:ct * 16 + s + 1]).then_inc(S["AEX"], 1)
                        if th == 0:
                            act.memzero(bsb[:, 0:1]).then_inc(S["CIN"], 1)
                        else:
                            act.wait_ge(S["SCP"], scp_need(si, b, s, ct))
                            act.copy(bsb[:, 0:1], scst[ct][:, s:s + 1]).then_inc(S["CIN"], 1)
                        cin += 1

        @blk.vector
        def _(v):
            v.memset(onesr[:], 1.0).then_inc(S["INIT"], 1)
            v.wait_ge(S["AEXP"], 1)
            v.tensor_scalar_mul(acol[:], acol[:], -1.0).then_inc(S["NEG"], 1)
            for si, (b, th) in enumerate(SEGS):
                for ct in range(2):
                    v.wait_ge(S[f"LD{ct}"], 32 * (si + 1))
                    v.tensor_tensor(gt[ct][:], gt[ct][:], dtt[ct][:], ALU.mult)
                for s in range(16):
                    st_ = si * 16 + s
                    for ct in range(2):
                        v.wait_ge(S["PEV"], 2 * st_ + 1)
                        v.tensor_tensor(bsb[:, 1:TH + 1], gt[ct][:], brep[:],
                                        ALU.mult).then_inc(S["BSD"], 1)
                        v.wait_ge(S["AEX"], 2 * st_ + ct + 1)
                        v.wait_ge(S["CIN"], 2 * st_ + ct + 1)
                        if st_ >= 1:
                            v.wait_ge(S["GD"], 2 * st_ + ct)
                            need = scp_through.get(prev_pos(si, s, ct))
                            if need:
                                v.wait_ge(S["SCP"], need)
                        v.tensor_tensor_scan(hs[ct][:], abuf[:, 0:TH + 1], bsb[:, 0:TH + 1],
                                             0.0, ALU.mult, ALU.add).then_inc(S["SC"], 1)

        @blk.tensor
        def _(te):
            te.wait_ge(S["INIT"], 1)
            rounds = 0
            for si, (b, th) in enumerate(SEGS):
                for s in range(16):
                    st_ = si * 16 + s
                    te.wait_ge(S["BRL"], 32 * (st_ + 1))
                    for rep, row in ((0, brow), (1, crow)):
                        if rounds >= 2:
                            te.wait_ge(S["PEV"], rounds - 1)
                        for i in range(NCH):
                            ins = te.matmul(ps[(rounds % 2) * 4 + i][:], onesr[:, :],
                                            row[:, i * 512:(i + 1) * 512],
                                            start=True, stop=True)
                            if i == NCH - 1:
                                ins.then_inc(S["PMM"], 1)
                        rounds += 1
    return nc



def build_l0back():
    nc = bass.Bass("TRN2", target_bir_lowering=False)
    ys_c = nc.dram_tensor("ys_c", [D_INNER, TPC], F32, kind="ExternalInput").ap()
    xc_c = nc.dram_tensor("xc_c", [D_INNER, TPC], F32, kind="ExternalInput").ap()
    z_c = nc.dram_tensor("z_c", [D_INNER, TPC], F32, kind="ExternalInput").ap()
    xorig = nc.dram_tensor("xorig", [TPC, D], F32, kind="ExternalInput").ap()
    D_t = nc.dram_tensor("D_t", [128, 16], F32, kind="ExternalInput").ap()
    opw = nc.dram_tensor("opw", [D_INNER, D], F32, kind="ExternalInput").ap()
    n2w = nc.dram_tensor("n2w", [1, D], F32, kind="ExternalInput").ap()
    ident = nc.dram_tensor("ident", [128, 128], BF16, kind="ExternalInput").ap()
    o_xin1 = nc.dram_tensor("o_xin1", [TPC, D], F32, kind="ExternalOutput").ap()
    o_htT = nc.dram_tensor("o_htT", [D, TPC], BF16, kind="ExternalOutput").ap()

    NTT = TPC // 128

    with ExitStack() as ctx:
        sb = lambda name, shape, dt=F32: ctx.enter_context(nc.sbuf_tensor(name, shape, dt))
        identt = sb("identt", [128, 128], BF16)
        dcol = sb("dcol", [128, 16])
        n2row = sb("n2row", [1, D])
        n2rep = sb("n2rep", [128, D])
        onesr = sb("onesr", [1, 128])
        opwb = sb("opwb", [128, 16 * 1024], BF16)
        ysb = [sb(f"ysb{p}", [128, TPC]) for p in range(2)]
        xcb_ = [sb(f"xcb_{p}", [128, TPC]) for p in range(2)]
        zb = [sb(f"zb{p}", [128, TPC]) for p in range(2)]
        y1b = sb("y1b", [128, TPC])
        szb = sb("szb", [128, TPC])
        ytot = [sb(f"ytot{k}", [128, TPC], BF16) for k in range(16)]
        xob = [sb(f"xob{p}", [128, D]) for p in range(2)]
        xin1 = [sb(f"xin1{p}", [128, D]) for p in range(2)]
        s2st = sb("s2st", [128, NTT])
        s2tmp = sb("s2tmp", [128, NTT])
        r2 = sb("r2", [128, NTT])
        fla = sb("fla", [128, 256])
        flb = sb("flb", [128, 256])
        sqscr = sb("sqscr", [128, D])
        htbf = [sb(f"htbf{p}", [128, D], BF16) for p in range(2)]
        htst = [sb(f"htst{p}", [128, 128], BF16) for p in range(2)]

        pso = [ctx.enter_context(nc.psum_tensor(f"pso{i}", [128, 1024], F32))
               for i in range(2)]
        psb = ctx.enter_context(nc.psum_tensor("psb", [128, 1024], F32))
        psT = [ctx.enter_context(nc.psum_tensor(f"psT{i}", [128, 128], BF16))
               for i in range(2)]

        S = _mk(ctx, nc, ["LW", "LC0", "LC1", "LO0", "LO1", "LI0", "LI1",
                          "Y1", "SZ", "YT", "MM", "RES", "SQ2", "NRM", "BCR",
                          "STI0", "STI1", "HT", "TP", "TE", "STH0", "STH1", "INIT"])
        blk = ctx.enter_context(nc.Block())

        @blk.sync
        def _(sync):
            for ct in range(16):
                if ct >= 2:
                    sync.wait_ge(S["YT"], ct - 1)
                p = ct % 2
                sync.dma_start(out=ysb[p][:], in_=ys_c[ct * 128:(ct + 1) * 128, :]).then_inc(S[f"LC{p}"], 16)
                sync.dma_start(out=xcb_[p][:], in_=xc_c[ct * 128:(ct + 1) * 128, :]).then_inc(S[f"LC{p}"], 16)
                sync.dma_start(out=zb[p][:], in_=z_c[ct * 128:(ct + 1) * 128, :]).then_inc(S[f"LC{p}"], 16)
            for tt in range(NTT):
                if tt >= 2:
                    sync.wait_ge(S["RES"], tt - 1)
                sync.dma_start(out=xob[tt % 2][:], in_=xorig[tt * 128:(tt + 1) * 128, :]).then_inc(S[f"LO{tt % 2}"], 16)
            sync.wait_ge(S["RES"], NTT)
            for tt in range(NTT):
                if tt >= 2:
                    sync.wait_ge(S["HT"], tt - 1)
                sync.wait_ge(S[f"STI{tt % 2}"], 16 * (tt // 2 + 1))
                sync.dma_start(out=xob[tt % 2][:], in_=o_xin1[tt * 128:(tt + 1) * 128, :]).then_inc(S[f"LI{tt % 2}"], 16)


        @blk.gpsimd
        def _(g):
            g.dma_start(out=dcol[:], in_=D_t[:, :]).then_inc(S["LW"], 16)
            g.dma_start(out=n2row[:], in_=n2w[:, :]).then_inc(S["LW"], 16)
            g.dma_start(out=identt[:], in_=ident[:, :]).then_inc(S["LW"], 16)
            for k in range(16):
                g.dma_start(out=opwb[:, k * 1024:(k + 1) * 1024],
                            in_=opw[k * 128:(k + 1) * 128, :]).then_inc(S["LW"], 16)
            for tt in range(NTT):
                g.wait_ge(S["RES"], tt + 1)
                g.dma_start(out=o_xin1[tt * 128:(tt + 1) * 128, :],
                            in_=xin1[tt % 2][:]).then_inc(S[f"STI{tt % 2}"], 16)
            for tt in range(NTT):
                for t in range(8):
                    gidx = tt * 8 + t
                    g.wait_ge(S["TE"], gidx + 1)
                    g.dma_start(out=o_htT[t * 128:(t + 1) * 128, tt * 128:(tt + 1) * 128],
                                in_=htst[gidx % 2][:]).then_inc(S[f"STH{gidx % 2}"], 16)
            g.wait_ge(S["STH0"], 16 * 64)
            g.wait_ge(S["STH1"], 16 * 64)
            g.wait_ge(S["STI0"], 16 * 8)
            g.wait_ge(S["STI1"], 16 * 8)

        LW_ALL = 16 * 19

        @blk.vector
        def _(v):
            v.memset(onesr[:], 1.0).then_inc(S["INIT"], 1)
            for ct in range(16):
                p = ct % 2
                v.wait_ge(S[f"LC{p}"], 48 * (ct // 2 + 1))
                v.wait_ge(S["LW"], LW_ALL)
                if ct >= 1:
                    v.wait_ge(S["SZ"], ct)
                v.scalar_tensor_tensor(y1b[:], xcb_[p][:], dcol[:, ct:ct + 1],
                                       ysb[p][:], ALU.mult, ALU.add).then_inc(S["Y1"], 1)
                v.wait_ge(S["SZ"], ct + 1)
                v.tensor_tensor(ytot[ct][:], y1b[:], szb[:], ALU.mult).then_inc(S["YT"], 1)
            for tt in range(NTT):
                p = tt % 2
                v.wait_ge(S["MM"], 32 * (tt + 1))
                v.wait_ge(S[f"LO{p}"], 16 * (tt // 2 + 1))
                if tt >= 2:
                    v.wait_ge(S[f"STI{p}"], 16 * (tt // 2))
                v.tensor_tensor(xin1[p][:, 0:512], pso[p][0:128, 0:512], xob[p][:, 0:512], ALU.add)
                v.tensor_tensor(xin1[p][:, 512:1024], pso[p][0:128, 512:1024],
                                xob[p][:, 512:1024], ALU.add).then_inc(S["RES"], 1)
            v.wait_ge(S["SQ2"], NTT)
            v.tensor_scalar(s2tmp[:], s2st[:], 1.0 / D, 1e-6, ALU.mult, ALU.add).then_inc(S["NRM"], 1)
            v.wait_ge(S["SQ2"], NTT + 1)
            v.reciprocal(r2[:], s2st[:]).then_inc(S["NRM"], 1)
            v.tensor_copy(flb[:], fla[:])
            for tt in range(NTT):
                p = tt % 2
                v.wait_ge(S[f"LI{p}"], 16 * (tt // 2 + 1))
                v.wait_ge(S["BCR"], 3)
                if tt >= 2:
                    v.wait_ge(S["TP"], 8 * (tt - 1))
                v.scalar_tensor_tensor(htbf[p][:], xob[p][:], r2[:, tt:tt + 1],
                                       n2rep[:], ALU.mult, ALU.mult).then_inc(S["HT"], 1)

        @blk.scalar
        def _(act):
            for ct in range(16):
                p = ct % 2
                act.wait_ge(S[f"LC{p}"], 48 * (ct // 2 + 1))
                if ct >= 1:
                    act.wait_ge(S["YT"], ct)
                act.activation(out=szb[:], in_=zb[p][:], func=AF.Silu).then_inc(S["SZ"], 1)
            for tt in range(NTT):
                act.wait_ge(S["RES"], tt + 1)
                act.activation(out=sqscr[:], in_=xin1[tt % 2][:], func=AF.Square,
                               accum_out=s2st[:, tt:tt + 1]).then_inc(S["SQ2"], 1)
            act.wait_ge(S["NRM"], 1)
            act.activation(out=s2st[:], in_=s2tmp[:], func=AF.Sqrt).then_inc(S["SQ2"], 1)
            act.wait_ge(S["BCR"], 2)
            act.copy(n2rep[:, 0:512], psb[0:128, 0:512])
            act.copy(n2rep[:, 512:1024], psb[0:128, 512:1024]).then_inc(S["BCR"], 1)
            for tt in range(NTT):
                for t in range(8):
                    gidx = tt * 8 + t
                    act.wait_ge(S["TP"], gidx + 1)
                    if gidx >= 2:
                        act.wait_ge(S[f"STH{gidx % 2}"], 16 * (gidx // 2))
                    act.copy(htst[gidx % 2][:], psT[gidx % 2][:]).then_inc(S["TE"], 1)

        @blk.tensor
        def _(te):
            te.wait_ge(S["INIT"], 1)
            te.wait_ge(S["LW"], LW_ALL)
            te.matmul(psb[0:128, 0:512], onesr[:, :], n2row[:, 0:512], start=True, stop=True)
            te.matmul(psb[0:128, 512:1024], onesr[:, :], n2row[:, 512:1024],
                      start=True, stop=True).then_inc(S["BCR"], 2)
            for tt in range(NTT):
                p = tt % 2
                if tt >= 2:
                    te.wait_ge(S["RES"], tt - 1)
                for k in range(16):
                    te.wait_ge(S["YT"], k + 1)
                    for h in range(2):
                        ins = te.matmul(pso[p][0:128, h * 512:(h + 1) * 512],
                                        ytot[k][:, tt * 128:(tt + 1) * 128],
                                        opwb[:, k * 1024 + h * 512:k * 1024 + (h + 1) * 512],
                                        start=(k == 0), stop=(k == 15))
                        if k == 15 and h == 1:
                            ins.then_inc(S["MM"], 32)
            for tt in range(NTT):
                te.wait_ge(S["HT"], tt + 1)
                for t in range(8):
                    gidx = tt * 8 + t
                    if gidx >= 2:
                        te.wait_ge(S["TE"], gidx - 1)
                    te.transpose(psT[gidx % 2][:], htbf[tt % 2][:, t * 128:(t + 1) * 128],
                                 identt[:]).then_inc(S["TP"], 1)
    return nc



GELU_A = 0.044715
GELU_2C = 2.0 * 0.7978845608028654


def build_l0moe():
    nc = bass.Bass("TRN2", target_bir_lowering=False)
    htT_d = nc.dram_tensor("htT", [D, TPC], BF16, kind="ExternalInput").ap()
    xin1_d = nc.dram_tensor("xin1", [TPC, D], F32, kind="ExternalInput").ap()
    xorig_d = nc.dram_tensor("xorig", [TPC, D], F32, kind="ExternalInput").ap()
    rw_d = nc.dram_tensor("rw", [D, 8], F32, kind="ExternalInput").ap()
    w1_d = nc.dram_tensor("w1", [8, D, 2048], F32, kind="ExternalInput").ap()
    b1_d = nc.dram_tensor("b1_t", [128, 8 * 16], F32, kind="ExternalInput").ap()
    w2_d = nc.dram_tensor("w2", [8, 2048, D], F32, kind="ExternalInput").ap()
    b2_d = nc.dram_tensor("b2", [8, D], F32, kind="ExternalInput").ap()
    ident = nc.dram_tensor("ident", [128, 128], BF16, kind="ExternalInput").ap()
    o_x1 = nc.dram_tensor("o_x1", [TPC, D], F32, kind="ExternalOutput").ap()

    NTT = TPC // 128
    NQ = 4
    HKC = 4
    NI = 8 * NQ
    NJ = NI * HKC * 4

    with ExitStack() as ctx:
        sb = lambda name, shape, dt=F32: ctx.enter_context(nc.sbuf_tensor(name, shape, dt))
        identt = sb("identt", [128, 128], BF16)
        rwb = sb("rwb", [128, 8 * 8], BF16)
        b1t = sb("b1t", [128, 8 * 16])
        b2b = sb("b2b", [8, D], BF16)
        htT = [sb(f"htT{k}", [128, TPC], BF16) for k in range(8)]
        gcol = sb("gcol", [128, NTT * 8])
        gbf = sb("gbf", [128, 8], BF16)
        gtT = sb("gtT", [8, TPC], BF16)
        esb = sb("esb", [128, 8])
        mcol = sb("mcol", [128, 1])
        e2c = sb("e2c", [128, 1])
        rden = sb("rden", [128, 1])
        msk = sb("msk", [128, 8])
        indb = sb("indb", [128, 8])
        w1sb = [sb(f"w1sb{p}", [128, 4 * 1024], BF16) for p in range(2)]
        w2sb = [sb(f"w2sb{p}", [128, 4 * 1024], BF16) for p in range(2)]
        hidq = [sb(f"hidq{kc}", [128, TPC], BF16) for kc in range(HKC)]
        usb = sb("usb", [128, 512])
        sqs = sb("sqs", [128, 512])
        vsb = sb("vsb", [128, 512])
        u2s = sb("u2s", [128, 512])
        gss = sb("gss", [128, 512])
        acc = [sb(f"acc{t}", [128, D]) for t in range(NTT)]
        xi = [sb(f"xi{p}", [128, D]) for p in range(2)]
        xo = [sb(f"xo{p}", [128, D]) for p in range(2)]
        t3 = [sb(f"t3{p}", [128, D]) for p in range(2)]
        mu = sb("mu", [128, 1])
        var = sb("var", [128, 1])
        vsc = sb("vsc", [128, 1])
        vsq = sb("vsq", [128, 1])
        rstd = sb("rstd", [128, 1])
        sqscr = sb("sqscr", [128, D])
        outt = [sb(f"outt{p}", [128, D]) for p in range(2)]
        fla = sb("fla", [128, 256])
        flb = sb("flb", [128, 256])

        psr = ctx.enter_context(nc.psum_tensor("psr", [128, 8], F32))
        psg = ctx.enter_context(nc.psum_tensor("psg", [8, 128], BF16))
        psh = [ctx.enter_context(nc.psum_tensor(f"psh{i}", [128, 512], F32)) for i in range(2)]
        pso = [ctx.enter_context(nc.psum_tensor(f"pso{i}", [128, 1024], F32)) for i in range(2)]

        S = _mk(ctx, nc, ["LW", "LH", "RMM", "RE", "GT", "GTT", "GTE", "GD",
                          "W1L0", "W1L1", "W2L0", "W2L1", "HMM", "GEA", "GEB",
                          "GEC", "HID", "OMM", "ACC", "BMM", "LX0", "LX1",
                          "SUM", "CTR", "VAR", "LNF", "STO0", "STO1", "AIN"])
        blk = ctx.enter_context(nc.Block())

        LW_ALL = 16 * 11

        def gelu_v(v, j):
            kc = (j % 16) // 4
            tb = j % 4
            v.wait_ge(S["HMM"], 8 * (j + 1))
            v.tensor_scalar_add(usb[:], psh[j % 2][:], _bcol_for(j))
            v.tensor_tensor(sqs[:], usb[:], usb[:], ALU.mult)
            v.tensor_scalar(vsb[:], sqs[:], GELU_A, 1.0, ALU.mult, ALU.add)
            v.tensor_tensor(u2s[:], vsb[:], usb[:], ALU.mult).then_inc(S["GEB"], 1)
            v.wait_ge(S["GEA"], j + 1)
            v.tensor_tensor(hidq[kc][:, tb * 512:(tb + 1) * 512], gss[:], usb[:],
                            ALU.mult).then_inc(S["HID"], 1)

        def _bcol_for(j):
            e = j // (NQ * HKC * 4)
            rem = j % (NQ * HKC * 4)
            q = rem // (HKC * 4)
            kc = (rem % (HKC * 4)) // 4
            return b1t[:, e * 16 + q * 4 + kc:e * 16 + q * 4 + kc + 1]

        @blk.sync
        def _(sync):
            for k in range(8):
                sync.dma_start(out=htT[k][:], in_=htT_d[k * 128:(k + 1) * 128, :]).then_inc(S["LH"], 16)
            for tt in range(NTT):
                p = tt % 2
                if tt >= 2:
                    sync.wait_ge(S["SUM"], tt - 1)
                sync.dma_start(out=xi[p][:], in_=xin1_d[tt * 128:(tt + 1) * 128, :]).then_inc(S[f"LX{p}"], 16)
                sync.dma_start(out=xo[p][:], in_=xorig_d[tt * 128:(tt + 1) * 128, :]).then_inc(S[f"LX{p}"], 16)

        @blk.gpsimd
        def _(g):
            g.dma_start(out=identt[:], in_=ident[:, :]).then_inc(S["LW"], 16)
            for k in range(8):
                g.dma_start(out=rwb[:, k * 8:(k + 1) * 8],
                            in_=rw_d[k * 128:(k + 1) * 128, :]).then_inc(S["LW"], 16)
            g.dma_start(out=b1t[:], in_=b1_d[:, :]).then_inc(S["LW"], 16)
            g.dma_start(out=b2b[:], in_=b2_d[:, :]).then_inc(S["LW"], 16)

            def casts(i):
                e, q = i // NQ, i % NQ
                p = i % 2
                if i >= 2:
                    g.wait_ge(S["HMM"], 128 * (i - 1))
                for kc in range(4):
                    c0 = q * 512 + kc * 128
                    g.dma_start(out=w1sb[p][:, kc * 1024:(kc + 1) * 1024],
                                in_=w1_d[e, :, c0:c0 + 128]
                                .rearrange("(k pp) c -> pp k c", pp=128)
                                ).then_inc(S[f"W1L{p}"], 16)
                if i >= 2:
                    g.wait_ge(S["OMM"], 128 * (i - 1))
                for kc in range(4):
                    g.dma_start(out=w2sb[p][:, kc * 1024:(kc + 1) * 1024],
                                in_=w2_d[e, q * 512 + kc * 128:q * 512 + (kc + 1) * 128, :]
                                ).then_inc(S[f"W2L{p}"], 16)

            casts(0)
            casts(1)
            for i in range(NI):
                if i + 2 < NI:
                    casts(i + 2)
            for tt in range(NTT):
                g.wait_ge(S["LNF"], tt + 1)
                g.dma_start(out=o_x1[tt * 128:(tt + 1) * 128, :],
                            in_=outt[tt % 2][:]).then_inc(S[f"STO{tt % 2}"], 16)
            g.wait_ge(S["STO0"], 16 * 8)
            g.wait_ge(S["STO1"], 16 * 8)

        @blk.vector
        def _(v):
            for t in range(NTT):
                v.memset(acc[t][:], 0.0)
            def fl():
                v.tensor_copy(flb[:], fla[:])

            v.memset(mcol[:], 0.0).then_inc(S["AIN"], 1)
            for tt in range(NTT):
                v.wait_ge(S["RMM"], tt + 1)
                v.reduce_max(mcol[:], psr[:, 0:8], axis=mybir.AxisListType.X,
                             negate=True).then_inc(S["GT"], 1)
                v.wait_ge(S["RE"], tt + 1)
                v.tensor_single_scalar(msk[:], esb[:], 1.0, ALU.is_ge)
                fl()
                v.scalar_tensor_tensor(indb[:], msk[:], -2.0, esb[:], ALU.mult, ALU.add)
                fl()
                v.reduce_max(e2c[:], indb[:], axis=mybir.AxisListType.X)
                fl()
                v.tensor_scalar(rden[:], e2c[:], 1.0, None, ALU.add)
                fl()
                v.reciprocal(rden[:], rden[:])
                fl()
                v.tensor_single_scalar(indb[:], esb[:], e2c[:, 0:1], ALU.is_ge)
                fl()
                v.scalar_tensor_tensor(gcol[:, tt * 8:(tt + 1) * 8], esb[:], rden[:, 0:1],
                                       indb[:], ALU.mult, ALU.mult)
                if tt >= 1:
                    v.wait_ge(S["GTT"], tt)
                fl()
                v.tensor_copy(gbf[:], gcol[:, tt * 8:(tt + 1) * 8]).then_inc(S["GD"], 1)
            v.wait_ge(S["GD"], NTT)
            for i in range(NI):
                e = i // NQ
                for jj in range(16):
                    gelu_v(v, i * 16 + jj)
                for tt in range(NTT):
                    v.wait_ge(S["OMM"], 128 * i + 8 * (tt + 1))
                    v.scalar_tensor_tensor(acc[tt][:], pso[tt % 2][:],
                                           gcol[:, tt * 8 + e:tt * 8 + e + 1],
                                           acc[tt][:], ALU.mult, ALU.add).then_inc(S["ACC"], 1)
            for tt in range(NTT):
                p = tt % 2
                v.wait_ge(S["BMM"], tt + 1)
                v.wait_ge(S[f"LX{p}"], 32 * (tt // 2 + 1))
                v.tensor_tensor(t3[p][:, 0:512], acc[tt][:, 0:512], psh[0][:], ALU.add)
                v.tensor_tensor(t3[p][:, 512:1024], acc[tt][:, 512:1024], psh[1][:], ALU.add)
                v.tensor_tensor(t3[p][:], t3[p][:], xi[p][:], ALU.add)
                v.tensor_tensor(t3[p][:], t3[p][:], xo[p][:], ALU.add).then_inc(S["SUM"], 1)
                v.tensor_reduce(mu[:], t3[p][:], mybir.AxisListType.X, ALU.add)
                fl()
                v.tensor_scalar_mul(mu[:], mu[:], 1.0 / D)
                fl()
                v.tensor_scalar(t3[p][:], t3[p][:], mu[:, 0:1], None,
                                ALU.subtract).then_inc(S["CTR"], 1)
                v.wait_ge(S["VAR"], 2 * tt + 1)
                v.tensor_scalar(vsc[:], var[:], 1.0 / D, 1e-5, ALU.mult,
                                ALU.add).then_inc(S["CTR"], 1)
                v.wait_ge(S["VAR"], 2 * tt + 2)
                v.reciprocal(rstd[:], vsq[:])
                if tt >= 2:
                    v.wait_ge(S[f"STO{p}"], 16 * (tt // 2))
                fl()
                v.tensor_scalar_mul(outt[p][:], t3[p][:], rstd[:, 0:1]).then_inc(S["LNF"], 1)

        @blk.scalar
        def _(act):
            for tt in range(NTT):
                act.wait_ge(S["GT"], tt + 1)
                act.activation(out=esb[:], in_=psr[:, 0:8], func=AF.Exp,
                               bias=mcol[:, 0:1]).then_inc(S["RE"], 1)
                act.wait_ge(S["GTT"], tt + 1)
                act.copy(gtT[0:8, tt * 128:(tt + 1) * 128], psg[0:8, :]).then_inc(S["GTE"], 1)
            for j in range(NJ):
                act.wait_ge(S["GEB"], j + 1)
                if j >= 1:
                    act.wait_ge(S["HID"], j)
                act.activation(out=gss[:], in_=u2s[:], func=AF.Sigmoid,
                               scale=GELU_2C).then_inc(S["GEA"], 1)
            for tt in range(NTT):
                act.wait_ge(S["CTR"], 2 * tt + 1)
                act.activation(out=sqscr[:], in_=t3[tt % 2][:], func=AF.Square,
                               accum_out=var[:, 0:1]).then_inc(S["VAR"], 1)
                act.wait_ge(S["CTR"], 2 * tt + 2)
                act.activation(out=vsq[:], in_=vsc[:], func=AF.Sqrt).then_inc(S["VAR"], 1)

        @blk.tensor
        def _(te):
            te.wait_ge(S["LW"], LW_ALL)
            te.wait_ge(S["LH"], 16 * 8)
            te.wait_ge(S["AIN"], 1)
            for tt in range(NTT):
                if tt >= 1:
                    te.wait_ge(S["RE"], tt)
                for k in range(8):
                    ins = te.matmul(psr[:, 0:8], htT[k][:, tt * 128:(tt + 1) * 128],
                                    rwb[:, k * 8:(k + 1) * 8], start=(k == 0), stop=(k == 7))
                    if k == 7:
                        ins.then_inc(S["RMM"], 1)
                te.wait_ge(S["GD"], tt + 1)
                if tt >= 1:
                    te.wait_ge(S["GTE"], tt)
                te.transpose(psg[:, :], gbf[:, 0:8], identt[:]).then_inc(S["GTT"], 1)
            for i in range(NI):
                e, q = i // NQ, i % NQ
                p = i % 2
                te.wait_ge(S[f"W1L{p}"], 64 * (i // 2 + 1))
                for kc in range(HKC):
                    for tb in range(4):
                        j = i * 16 + kc * 4 + tb
                        if j >= 2:
                            te.wait_ge(S["GEB"], j - 1)
                        for k in range(8):
                            ins = te.matmul(psh[j % 2][:],
                                            w1sb[p][:, kc * 1024 + k * 128:kc * 1024 + (k + 1) * 128],
                                            htT[k][:, tb * 512:(tb + 1) * 512],
                                            start=(k == 0), stop=(k == 7))
                            if k == 7:
                                ins.then_inc(S["HMM"], 8)
                te.wait_ge(S[f"W2L{p}"], 64 * (i // 2 + 1))
                te.wait_ge(S["HID"], 16 * (i + 1))
                for tt in range(NTT):
                    gidx = NTT * i + tt
                    if gidx >= 2:
                        te.wait_ge(S["ACC"], gidx - 1)
                    for kc in range(HKC):
                        for h in range(2):
                            ins = te.matmul(pso[tt % 2][:, h * 512:(h + 1) * 512],
                                            hidq[kc][:, tt * 128:(tt + 1) * 128],
                                            w2sb[p][:, kc * 1024 + h * 512:kc * 1024 + (h + 1) * 512],
                                            start=(kc == 0), stop=(kc == 3))
                            if kc == 3 and h == 1:
                                ins.then_inc(S["OMM"], 8)
            te.wait_ge(S["GEA"], NJ)
            te.wait_ge(S["GTE"], NTT)
            for tt in range(NTT):
                if tt >= 1:
                    te.wait_ge(S["SUM"], tt)
                for h in range(2):
                    ins = te.matmul(psh[h][:], gtT[0:8, tt * 128:(tt + 1) * 128],
                                    b2b[0:8, h * 512:(h + 1) * 512], start=True, stop=True)
                    if h == 1:
                        ins.then_inc(S["BMM"], 1)
    return nc



def build_l0ffn():
    nc = bass.Bass("TRN2", target_bir_lowering=False)
    x1_d = nc.dram_tensor("x1", [TPC, D], F32, kind="ExternalInput").ap()
    w1_d = nc.dram_tensor("w1", [8, D, 4096], F32, kind="ExternalInput").ap()
    b1_d = nc.dram_tensor("b1_t", [128, 8 * 32], F32, kind="ExternalInput").ap()
    w2_d = nc.dram_tensor("w2", [8, 4096, D], F32, kind="ExternalInput").ap()
    b2_d = nc.dram_tensor("b2", [8, D], F32, kind="ExternalInput").ap()
    ident = nc.dram_tensor("ident", [128, 128], BF16, kind="ExternalInput").ap()
    o_x2 = nc.dram_tensor("o_x2", [TPC, D], F32, kind="ExternalOutput").ap()

    NTT = TPC // 128
    NQ = 8
    HKC = 4
    NI = 8 * NQ
    with ExitStack() as ctx:
        sb = lambda name, shape, dt=F32: ctx.enter_context(nc.sbuf_tensor(name, shape, dt))
        identt = sb("identt", [128, 128], BF16)
        b1t = sb("b1t", [128, 8 * 32])
        b2b = sb("b2b", [8, D], BF16)
        x1T = [sb(f"x1T{k}", [128, TPC], BF16) for k in range(8)]
        xtile = [sb(f"xtile{p}", [128, D]) for p in range(2)]
        xbf = [sb(f"xbf{p}", [128, D], BF16) for p in range(2)]
        gcol = sb("gcol", [128, NTT * 8])
        gbf = sb("gbf", [128, 8], BF16)
        gtT = sb("gtT", [8, TPC], BF16)
        vcol = sb("vcol", [128, 1])
        f1c = sb("f1c", [128, 1])
        uc = sb("uc", [128, 1])
        mfc = sb("mfc", [128, 1])
        ngc = sb("ngc", [128, 1])
        nic = sb("nic", [128, 1])
        mm_ = sb("mm_", [128, 1])
        tmpc = sb("tmpc", [128, 1])
        fla = sb("fla", [128, 256])
        flb = sb("flb", [128, 256])
        w1sb = [sb(f"w1sb{p}", [128, 4 * 1024], BF16) for p in range(2)]
        w2sb = [sb(f"w2sb{p}", [128, 4 * 1024], BF16) for p in range(2)]
        hidq = [sb(f"hidq{kc}", [128, TPC], BF16) for kc in range(HKC)]
        acc = [sb(f"acc{t}", [128, D]) for t in range(NTT)]
        t3 = [sb(f"t3{p}", [128, D]) for p in range(2)]
        mu = sb("mu", [128, 1])
        var = sb("var", [128, 1])
        vsc = sb("vsc", [128, 1])
        vsq = sb("vsq", [128, 1])
        rstd = sb("rstd", [128, 1])
        sqscr = sb("sqscr", [128, D])
        outt = [sb(f"outt{p}", [128, D]) for p in range(2)]

        psg = ctx.enter_context(nc.psum_tensor("psg", [8, 128], BF16))
        psT = [ctx.enter_context(nc.psum_tensor(f"psT{i}", [128, 128], BF16)) for i in range(2)]
        psh = [ctx.enter_context(nc.psum_tensor(f"psh{i}", [128, 512], F32)) for i in range(2)]
        pso = [ctx.enter_context(nc.psum_tensor("pso0", [128, 1024], F32))]

        S = _mk(ctx, nc, ["LW", "LX0", "LX1", "XB", "TP", "TE", "GH", "GTT",
                          "GTE", "GD", "W1L0", "W1L1", "W2L0", "W2L1", "HMM",
                          "HEV", "OMM", "ACC", "BMM", "LR0", "LR1", "SUM",
                          "CTR", "VAR", "LNF", "STO0", "STO1"])
        blk = ctx.enter_context(nc.Block())

        @blk.sync
        def _(sync):
            for tt in range(NTT):
                p = tt % 2
                if tt >= 2:
                    sync.wait_ge(S["XB"], tt - 1)
                    sync.wait_ge(S["GH"], tt - 1)
                sync.dma_start(out=xtile[p][:], in_=x1_d[tt * 128:(tt + 1) * 128, :]).then_inc(S[f"LX{p}"], 16)
            sync.wait_ge(S["XB"], NTT)
            sync.wait_ge(S["GH"], NTT)
            for tt in range(NTT):
                p = tt % 2
                if tt >= 2:
                    sync.wait_ge(S["SUM"], tt - 1)
                sync.dma_start(out=xtile[p][:], in_=x1_d[tt * 128:(tt + 1) * 128, :]).then_inc(S[f"LR{p}"], 16)

        @blk.gpsimd
        def _(g):
            g.dma_start(out=identt[:], in_=ident[:, :]).then_inc(S["LW"], 16)
            g.dma_start(out=b1t[:], in_=b1_d[:, :]).then_inc(S["LW"], 16)
            g.dma_start(out=b2b[:], in_=b2_d[:, :]).then_inc(S["LW"], 16)

            def casts(i):
                e, q = i // NQ, i % NQ
                p = i % 2
                if i >= 2:
                    g.wait_ge(S["HMM"], 128 * (i - 1))
                for kc in range(4):
                    c0 = q * 512 + kc * 128
                    g.dma_start(out=w1sb[p][:, kc * 1024:(kc + 1) * 1024],
                                in_=w1_d[e, :, c0:c0 + 128]
                                .rearrange("(k pp) c -> pp k c", pp=128)
                                ).then_inc(S[f"W1L{p}"], 16)
                if i >= 2:
                    g.wait_ge(S["OMM"], 128 * (i - 1))
                for kc in range(4):
                    g.dma_start(out=w2sb[p][:, kc * 1024:(kc + 1) * 1024],
                                in_=w2_d[e, q * 512 + kc * 128:q * 512 + (kc + 1) * 128, :]
                                ).then_inc(S[f"W2L{p}"], 16)

            casts(0)
            casts(1)
            for i in range(NI):
                if i + 2 < NI:
                    casts(i + 2)
            for tt in range(NTT):
                g.wait_ge(S["LNF"], tt + 1)
                g.dma_start(out=o_x2[tt * 128:(tt + 1) * 128, :],
                            in_=outt[tt % 2][:]).then_inc(S[f"STO{tt % 2}"], 16)
            g.wait_ge(S["STO0"], 16 * 8)
            g.wait_ge(S["STO1"], 16 * 8)

        @blk.vector
        def _(v):
            def fl():
                v.tensor_copy(flb[:], fla[:])

            for t in range(NTT):
                v.memset(acc[t][:], 0.0)
            for tt in range(NTT):
                p = tt % 2
                v.wait_ge(S[f"LX{p}"], 16 * (tt // 2 + 1))
                if tt >= 2:
                    v.wait_ge(S["TP"], 8 * (tt - 1))
                v.tensor_copy(xbf[p][:], xtile[p][:]).then_inc(S["XB"], 1)
                v.tensor_tensor(vcol[:], xtile[p][:, 0:1], xtile[p][:, 1:2], ALU.add)
                fl()
                v.tensor_scalar(tmpc[:], vcol[:], 1.0 / 8.0, MAGIC, ALU.mult, ALU.add)
                fl()
                v.tensor_scalar(f1c[:], tmpc[:], MAGIC, None, ALU.subtract)
                v.tensor_scalar_mul(tmpc[:], vcol[:], 1.0 / 8.0)
                fl()
                v.tensor_tensor(ngc[:], f1c[:], tmpc[:], ALU.is_gt)
                fl()
                v.tensor_tensor(f1c[:], f1c[:], ngc[:], ALU.subtract)
                fl()
                v.scalar_tensor_tensor(uc[:], f1c[:], -8.0, vcol[:], ALU.mult, ALU.add)
                fl()
                v.tensor_scalar(tmpc[:], uc[:], 1.0, MAGIC, ALU.mult, ALU.add)
                fl()
                v.tensor_scalar(mfc[:], tmpc[:], MAGIC, None, ALU.subtract)
                fl()
                v.tensor_tensor(ngc[:], mfc[:], uc[:], ALU.is_gt)
                fl()
                v.tensor_tensor(mfc[:], mfc[:], ngc[:], ALU.subtract)
                v.tensor_single_scalar(ngc[:], vcol[:], 0.0, ALU.is_lt)
                fl()
                v.tensor_tensor(nic[:], uc[:], mfc[:], ALU.is_gt)
                fl()
                v.tensor_tensor(nic[:], nic[:], ngc[:], ALU.mult)
                fl()
                v.tensor_tensor(mm_[:], mfc[:], nic[:], ALU.add)
                fl()
                v.tensor_single_scalar(tmpc[:], mm_[:], 8.0, ALU.is_ge)
                fl()
                v.scalar_tensor_tensor(mm_[:], tmpc[:], -8.0, mm_[:], ALU.mult, ALU.add)
                fl()
                for e in range(8):
                    em1 = (e - 1) % 8
                    v.tensor_single_scalar(tmpc[:], mm_[:], float(e), ALU.is_equal)
                    v.tensor_single_scalar(ngc[:], mm_[:], float(em1), ALU.is_equal)
                    fl()
                    v.tensor_tensor(tmpc[:], tmpc[:], ngc[:], ALU.add)
                    fl()
                    v.tensor_scalar_mul(gcol[:, tt * 8 + e:tt * 8 + e + 1], tmpc[:], 0.5)
                if tt >= 1:
                    v.wait_ge(S["GTT"], tt)
                fl()
                v.tensor_copy(gbf[:], gcol[:, tt * 8:(tt + 1) * 8]).then_inc(S["GH"], 1)
            for i in range(NI):
                e = i // NQ
                for tt in range(NTT):
                    v.wait_ge(S["OMM"], 128 * i + 8 * (tt + 1))
                    v.scalar_tensor_tensor(acc[tt][:], pso[0][:],
                                           gcol[:, tt * 8 + e:tt * 8 + e + 1],
                                           acc[tt][:], ALU.mult, ALU.add).then_inc(S["ACC"], 1)
            for tt in range(NTT):
                p = tt % 2
                v.wait_ge(S["BMM"], tt + 1)
                v.wait_ge(S[f"LR{p}"], 16 * (tt // 2 + 1))
                v.tensor_tensor(t3[p][:, 0:512], acc[tt][:, 0:512], psh[0][:], ALU.add)
                v.tensor_tensor(t3[p][:, 512:1024], acc[tt][:, 512:1024], psh[1][:], ALU.add)
                v.tensor_tensor(t3[p][:], t3[p][:], xtile[p][:], ALU.add).then_inc(S["SUM"], 1)
                v.tensor_reduce(mu[:], t3[p][:], mybir.AxisListType.X, ALU.add)
                fl()
                v.tensor_scalar_mul(mu[:], mu[:], 1.0 / D)
                fl()
                v.tensor_scalar(t3[p][:], t3[p][:], mu[:, 0:1], None,
                                ALU.subtract).then_inc(S["CTR"], 1)
                v.wait_ge(S["VAR"], 2 * tt + 1)
                v.tensor_scalar(vsc[:], var[:], 1.0 / D, 1e-5, ALU.mult,
                                ALU.add).then_inc(S["CTR"], 1)
                v.wait_ge(S["VAR"], 2 * tt + 2)
                v.reciprocal(rstd[:], vsq[:])
                if tt >= 2:
                    v.wait_ge(S[f"STO{p}"], 16 * (tt // 2))
                fl()
                v.tensor_scalar_mul(outt[p][:], t3[p][:], rstd[:, 0:1]).then_inc(S["LNF"], 1)

        @blk.scalar
        def _(act):
            for tt in range(NTT):
                for t in range(8):
                    gidx = tt * 8 + t
                    act.wait_ge(S["TP"], gidx + 1)
                    act.copy(x1T[t][:, tt * 128:(tt + 1) * 128], psT[gidx % 2][:]).then_inc(S["TE"], 1)
                act.wait_ge(S["GTT"], tt + 1)
                act.copy(gtT[0:8, tt * 128:(tt + 1) * 128], psg[0:8, :]).then_inc(S["GTE"], 1)
            for i in range(NI):
                e, q = i // NQ, i % NQ
                for kc in range(HKC):
                    for tb in range(4):
                        j = i * 16 + kc * 4 + tb
                        bcol = b1t[:, e * 32 + q * 4 + kc:e * 32 + q * 4 + kc + 1]
                        act.wait_ge(S["HMM"], 8 * (j + 1))
                        act.activation(out=hidq[kc][:, tb * 512:(tb + 1) * 512],
                                       in_=psh[j % 2][:], func=AF.Relu,
                                       bias=bcol).then_inc(S["HEV"], 1)
            for tt in range(NTT):
                act.wait_ge(S["CTR"], 2 * tt + 1)
                act.activation(out=sqscr[:], in_=t3[tt % 2][:], func=AF.Square,
                               accum_out=var[:, 0:1]).then_inc(S["VAR"], 1)
                act.wait_ge(S["CTR"], 2 * tt + 2)
                act.activation(out=vsq[:], in_=vsc[:], func=AF.Sqrt).then_inc(S["VAR"], 1)

        @blk.tensor
        def _(te):
            te.wait_ge(S["LW"], 48)
            for tt in range(NTT):
                te.wait_ge(S["XB"], tt + 1)
                for t in range(8):
                    gidx = tt * 8 + t
                    if gidx >= 2:
                        te.wait_ge(S["TE"], gidx - 1)
                    te.transpose(psT[gidx % 2][:], xbf[tt % 2][:, t * 128:(t + 1) * 128],
                                 identt[:]).then_inc(S["TP"], 1)
                te.wait_ge(S["GH"], tt + 1)
                if tt >= 1:
                    te.wait_ge(S["GTE"], tt)
                te.transpose(psg[:, :], gbf[:, 0:8], identt[:]).then_inc(S["GTT"], 1)
            for i in range(NI):
                e, q = i // NQ, i % NQ
                p = i % 2
                te.wait_ge(S[f"W1L{p}"], 64 * (i // 2 + 1))
                te.wait_ge(S["TE"], NTT * 8)
                for kc in range(HKC):
                    for tb in range(4):
                        j = i * 16 + kc * 4 + tb
                        if j >= 2:
                            te.wait_ge(S["HEV"], j - 1)
                        for k in range(8):
                            ins = te.matmul(psh[j % 2][:],
                                            w1sb[p][:, kc * 1024 + k * 128:kc * 1024 + (k + 1) * 128],
                                            x1T[k][:, tb * 512:(tb + 1) * 512],
                                            start=(k == 0), stop=(k == 7))
                            if k == 7:
                                ins.then_inc(S["HMM"], 8)
                te.wait_ge(S[f"W2L{p}"], 64 * (i // 2 + 1))
                te.wait_ge(S["HEV"], 16 * (i + 1))
                for tt in range(NTT):
                    gidx = NTT * i + tt
                    if gidx >= 1:
                        te.wait_ge(S["ACC"], gidx)
                    for kc in range(HKC):
                        for h in range(2):
                            ins = te.matmul(pso[0][:, h * 512:(h + 1) * 512],
                                            hidq[kc][:, tt * 128:(tt + 1) * 128],
                                            w2sb[p][:, kc * 1024 + h * 512:kc * 1024 + (h + 1) * 512],
                                            start=(kc == 0), stop=(kc == 3))
                            if kc == 3 and h == 1:
                                ins.then_inc(S["OMM"], 8)
            te.wait_ge(S["GTE"], NTT)
            te.wait_ge(S["HEV"], 16 * NI)
            for tt in range(NTT):
                if tt >= 1:
                    te.wait_ge(S["SUM"], tt)
                for h in range(2):
                    ins = te.matmul(psh[h][:], gtT[0:8, tt * 128:(tt + 1) * 128],
                                    b2b[0:8, h * 512:(h + 1) * 512], start=True, stop=True)
                    if h == 1:
                        ins.then_inc(S["BMM"], 1)
    return nc



def build_qkv():
    nc = bass.Bass("TRN2", target_bir_lowering=False)
    x2_d = nc.dram_tensor("x2", [TPC, D], F32, kind="ExternalInput").ap()
    wq_d = nc.dram_tensor("wqT", [D, D], F32, kind="ExternalInput").ap()
    wk_d = nc.dram_tensor("wkT", [D, D], F32, kind="ExternalInput").ap()
    wv_d = nc.dram_tensor("wvT", [D, D], F32, kind="ExternalInput").ap()
    ident = nc.dram_tensor("ident", [128, 128], BF16, kind="ExternalInput").ap()
    o_qT = nc.dram_tensor("o_qT", [D, TPC], BF16, kind="ExternalOutput").ap()
    o_kT = nc.dram_tensor("o_kT", [D, TPC], BF16, kind="ExternalOutput").ap()
    o_vT = nc.dram_tensor("o_vT", [D, TPC], BF16, kind="ExternalOutput").ap()

    NTT = TPC // 128
    with ExitStack() as ctx:
        sb = lambda name, shape, dt=F32: ctx.enter_context(nc.sbuf_tensor(name, shape, dt))
        identt = sb("identt", [128, 128], BF16)
        xtile = [sb(f"xtile{p}", [128, D]) for p in range(2)]
        xbf = [sb(f"xbf{p}", [128, D], BF16) for p in range(2)]
        x2T = [sb(f"x2T{k}", [128, TPC], BF16) for k in range(8)]
        wsb = [sb(f"wsb{w}", [128, 8 * 1024], BF16) for w in range(3)]
        outb = [sb(f"outb{p}", [128, TPC], BF16) for p in range(2)]

        psT = [ctx.enter_context(nc.psum_tensor(f"psT{i}", [128, 128], BF16)) for i in range(2)]
        psm = [ctx.enter_context(nc.psum_tensor(f"psm{i}", [128, 512], F32)) for i in range(4)]

        S = _mk(ctx, nc, ["LW", "LX0", "LX1", "XB", "TP", "TE", "MM", "EV",
                          "STO0", "STO1"])
        blk = ctx.enter_context(nc.Block())
        OUTS = [o_qT, o_kT, o_vT]

        @blk.sync
        def _(sync):
            for tt in range(NTT):
                p = tt % 2
                if tt >= 2:
                    sync.wait_ge(S["XB"], tt - 1)
                sync.dma_start(out=xtile[p][:], in_=x2_d[tt * 128:(tt + 1) * 128, :]).then_inc(S[f"LX{p}"], 16)

        @blk.gpsimd
        def _(g):
            g.dma_start(out=identt[:], in_=ident[:, :]).then_inc(S["LW"], 16)
            for w, wd in enumerate([wq_d, wk_d, wv_d]):
                for k in range(8):
                    g.dma_start(out=wsb[w][:, k * 1024:(k + 1) * 1024],
                                in_=wd[k * 128:(k + 1) * 128, :]).then_inc(S["LW"], 16)
            for u in range(24):
                w, fc = u // 8, u % 8
                g.wait_ge(S["EV"], 4 * (u + 1))
                g.dma_start(out=OUTS[w][fc * 128:(fc + 1) * 128, :],
                            in_=outb[u % 2][:]).then_inc(S[f"STO{u % 2}"], 16)
            g.wait_ge(S["STO0"], 16 * 12)
            g.wait_ge(S["STO1"], 16 * 12)

        @blk.vector
        def _(v):
            for tt in range(NTT):
                p = tt % 2
                v.wait_ge(S[f"LX{p}"], 16 * (tt // 2 + 1))
                if tt >= 2:
                    v.wait_ge(S["TP"], 8 * (tt - 1))
                v.tensor_copy(xbf[p][:], xtile[p][:]).then_inc(S["XB"], 1)

        @blk.scalar
        def _(act):
            for tt in range(NTT):
                for t in range(8):
                    gidx = tt * 8 + t
                    act.wait_ge(S["TP"], gidx + 1)
                    act.copy(x2T[t][:, tt * 128:(tt + 1) * 128], psT[gidx % 2][:]).then_inc(S["TE"], 1)
            for u in range(24):
                w = u // 8
                scale = 0.125 if w == 0 else 1.0
                for blkk in range(4):
                    act.wait_ge(S["MM"], 4 * u + blkk + 1)
                    if u >= 2:
                        act.wait_ge(S[f"STO{u % 2}"], 16 * (u // 2))
                    act.activation(out=outb[u % 2][:, blkk * 512:(blkk + 1) * 512],
                                   in_=psm[blkk][:], func=AF.Copy,
                                   scale=scale).then_inc(S["EV"], 1)

        @blk.tensor
        def _(te):
            te.wait_ge(S["LW"], 16 * 25)
            for tt in range(NTT):
                te.wait_ge(S["XB"], tt + 1)
                for t in range(8):
                    gidx = tt * 8 + t
                    if gidx >= 2:
                        te.wait_ge(S["TE"], gidx - 1)
                    te.transpose(psT[gidx % 2][:], xbf[tt % 2][:, t * 128:(t + 1) * 128],
                                 identt[:]).then_inc(S["TP"], 1)
            te.wait_ge(S["TE"], NTT * 8)
            for u in range(24):
                w, fc = u // 8, u % 8
                for blkk in range(4):
                    if 4 * u + blkk >= 4:
                        te.wait_ge(S["EV"], 4 * u + blkk - 3)
                    for k in range(8):
                        ins = te.matmul(psm[blkk][:],
                                        wsb[w][:, k * 1024 + fc * 128:k * 1024 + (fc + 1) * 128],
                                        x2T[k][:, blkk * 512:(blkk + 1) * 512],
                                        start=(k == 0), stop=(k == 7))
                        if k == 7:
                            ins.then_inc(S["MM"], 1)
    return nc



def build_attn():
    nc = bass.Bass("TRN2", target_bir_lowering=False)
    q_d = nc.dram_tensor("qT_h", [4, 64, L], BF16, kind="ExternalInput").ap()
    k_d = nc.dram_tensor("kT_h", [4, 64, L], BF16, kind="ExternalInput").ap()
    v_d = nc.dram_tensor("vT_h", [4, 64, L], BF16, kind="ExternalInput").ap()
    ident = nc.dram_tensor("ident", [128, 128], BF16, kind="ExternalInput").ap()
    o_ao = nc.dram_tensor("o_aoT", [4, 64, L], BF16, kind="ExternalOutput").ap()

    NQB = L // 512
    NBAT = NQB * (NQB + 1) // 2

    def gbat(u, qb, bat):
        return u * NBAT + qb * (qb + 1) // 2 + bat

    with ExitStack() as ctx:
        sb = lambda name, shape, dt=F32: ctx.enter_context(nc.sbuf_tensor(name, shape, dt))
        identt = sb("identt", [128, 128], BF16)
        qhat = sb("qhat", [65, L], BF16)
        khat = sb("khat", [65, L], BF16)
        vtt = sb("vtt", [64, L], BF16)
        vh = sb("vh", [128, 65 * 64], BF16)
        sq2 = sb("sq2", [64, L], BF16)
        qn2 = sb("qn2", [1, L], BF16)
        kn2 = sb("kn2", [1, L], BF16)
        kmx = sb("kmx", [1, 1])
        kmq = sb("kmq", [1, 1])
        mrow = sb("mrow", [1, L], BF16)
        atb = sb("atb", [128, 2048], BF16)
        ssb = sb("ssb", [128, 512])
        srow = sb("srow", [1, 512])
        rr = sb("rr", [1, 512])
        rrep = sb("rrep", [64, 512])
        aobuf = sb("aobuf", [64, L], BF16)
        onesr = sb("onesr", [1, 64])
        onescol = sb("onescol", [64, 1], BF16)

        psS = ctx.enter_context(nc.psum_tensor("psS", [128, 2048], F32))
        psO = ctx.enter_context(nc.psum_tensor("psO", [65, 512], F32))
        psB = ctx.enter_context(nc.psum_tensor("psB", [64, 512], F32))
        psN = ctx.enter_context(nc.psum_tensor("psN", [1, 512], F32))
        psT = ctx.enter_context(nc.psum_tensor("psT", [128, 64], BF16))

        S = _mk(ctx, nc, ["LW", "LQ", "LK", "LV", "INIT", "SQA", "NMM", "NEV",
                          "KMX", "MH", "VHE", "VTP", "SMM", "EXP", "MSK",
                          "VMM", "SEV", "SDM", "RCP", "BMM", "BEV", "AOB",
                          "STA", "UD"])
        blk = ctx.enter_context(nc.Block())

        @blk.sync
        def _(sync):
            sync.dma_start(out=identt[:], in_=ident[:, :]).then_inc(S["LW"], 16)
            for u in range(4):
                if u >= 1:
                    sync.wait_ge(S["VMM"], NBAT * u)
                sync.dma_start(out=qhat[0:64, :], in_=q_d[u]).then_inc(S["LQ"], 16)
                sync.dma_start(out=khat[0:64, :], in_=k_d[u]).then_inc(S["LK"], 16)
                sync.dma_start(out=vtt[0:64, :], in_=v_d[u]).then_inc(S["LV"], 16)

        @blk.gpsimd
        def _(g):
            for u in range(4):
                for qb in range(NQB):
                    gdiag = gbat(u, qb, qb)
                    g.wait_ge(S["EXP"], gdiag + 1)
                    for j in range(4):
                        kt = 4 * qb + j
                        base = qb * 512 - kt * 128
                        ins = g.affine_select(atb[:, j * 512:(j + 1) * 512],
                                              atb[:, j * 512:(j + 1) * 512],
                                              [[1, 512]], ALU.is_ge, 0.0,
                                              base=base, channel_multiplier=-1)
                        if j == 3:
                            ins.then_inc(S["MSK"], 1)
                    qi = u * NQB + qb
                    g.wait_ge(S["SEV"], qi + 1)
                    g.dma_start(out=srow[:], in_=ssb[64:65, :]).then_inc(S["SDM"], 16)
                g.wait_ge(S["AOB"], NQB * (u + 1))
                g.dma_start(out=o_ao[u], in_=aobuf[0:64, :]).then_inc(S["STA"], 16)
            g.wait_ge(S["STA"], 64)

        @blk.vector
        def _(v):
            v.memset(onesr[:], 1.0)
            v.memset(onescol[:], 1.0)
            v.memset(khat[64:65, :], 1.0)
            v.memset(vh[:], 1.0).then_inc(S["INIT"], 1)
            for u in range(4):
                v.wait_ge(S["NEV"], 32 * u + 32)
                v.reduce_max(kmx[:], kn2[0:1, :], axis=mybir.AxisListType.X).then_inc(S["KMX"], 1)
                v.wait_ge(S["SQA"], 4 * u + 4)
                v.tensor_scalar(qhat[64:65, :], mrow[0:1, :], kmq[0:1, 0:1], -1.0,
                                ALU.mult, ALU.mult).then_inc(S["MH"], 1)
                for qb in range(NQB):
                    qi = u * NQB + qb
                    v.wait_ge(S["SDM"], 16 * (qi + 1))
                    v.reciprocal(rr[:], srow[:]).then_inc(S["RCP"], 1)
                    v.wait_ge(S["BEV"], qi + 1)
                    if qb == 0 and u >= 1:
                        v.wait_ge(S["STA"], 16 * u)
                    v.tensor_tensor(aobuf[:, qb * 512:(qb + 1) * 512], psO[0:64, :],
                                    rrep[0:64, :], ALU.mult).then_inc(S["AOB"], 1)

        @blk.scalar
        def _(act):
            for u in range(4):
                act.wait_ge(S["LQ"], 16 * (u + 1))
                if u >= 1:
                    act.wait_ge(S["NMM"], 32 * u)
                act.activation(out=sq2[:], in_=qhat[0:64, :], func=AF.Square).then_inc(S["SQA"], 1)
                for blkk in range(16):
                    ni = 32 * u + blkk
                    act.wait_ge(S["NMM"], ni + 1)
                    src = psN[0:1, :] if blkk % 2 == 0 else psB[0:1, :]
                    act.copy(qn2[0:1, blkk * 512:(blkk + 1) * 512], src).then_inc(S["NEV"], 1)
                act.wait_ge(S["LK"], 16 * (u + 1))
                act.wait_ge(S["NMM"], 32 * u + 16)
                act.activation(out=sq2[:], in_=khat[0:64, :], func=AF.Square).then_inc(S["SQA"], 1)
                for blkk in range(16):
                    ni = 32 * u + 16 + blkk
                    act.wait_ge(S["NMM"], ni + 1)
                    src = psN[0:1, :] if blkk % 2 == 0 else psB[0:1, :]
                    act.copy(kn2[0:1, blkk * 512:(blkk + 1) * 512], src).then_inc(S["NEV"], 1)
                act.wait_ge(S["KMX"], u + 1)
                act.activation(out=kmq[:], in_=kmx[:], func=AF.Sqrt).then_inc(S["SQA"], 1)
                act.activation(out=mrow[:], in_=qn2[0:1, :], func=AF.Sqrt).then_inc(S["SQA"], 1)
                for ci in range(64):
                    vi = 64 * u + ci
                    act.wait_ge(S["VTP"], vi + 1)
                    act.copy(vh[0:128, ci * 65:ci * 65 + 64], psT[0:128, 0:64]).then_inc(S["VHE"], 1)
                for qb in range(NQB):
                    for bat in range(qb + 1):
                        gg = gbat(u, qb, bat)
                        act.wait_ge(S["SMM"], gg + 1)
                        if gg >= 1:
                            act.wait_ge(S["VMM"], gg)
                        nk = 4
                        act.activation(out=atb[:, 0:2048], in_=psS[:, 0:2048],
                                       func=AF.Exp).then_inc(S["EXP"], 1)
                    qi = u * NQB + qb
                    act.wait_ge(S["VMM"], gbat(u, qb, qb) + 1)
                    act.copy(ssb[64:65, :], psO[64:65, :]).then_inc(S["SEV"], 1)
                    act.wait_ge(S["BMM"], qi + 1)
                    act.copy(rrep[0:64, :], psB[0:64, :]).then_inc(S["BEV"], 1)

        @blk.tensor
        def _(te):
            te.wait_ge(S["LW"], 16)
            te.wait_ge(S["INIT"], 1)
            for u in range(4):
                te.wait_ge(S["SQA"], 4 * u + 1)
                if u >= 1:
                    te.wait_ge(S["BEV"], 16 * u)
                for blkk in range(16):
                    ni = 32 * u + blkk
                    dst = psN[0:1, :] if blkk % 2 == 0 else psB[0:1, :]
                    if ni >= 2:
                        te.wait_ge(S["NEV"], ni - 1)
                    te.matmul(dst, onescol[0:64, 0:1], sq2[0:64, blkk * 512:(blkk + 1) * 512],
                              start=True, stop=True).then_inc(S["NMM"], 1)
                te.wait_ge(S["SQA"], 4 * u + 2)
                for blkk in range(16):
                    ni = 32 * u + 16 + blkk
                    dst = psN[0:1, :] if blkk % 2 == 0 else psB[0:1, :]
                    te.wait_ge(S["NEV"], ni - 1)
                    te.matmul(dst, onescol[0:64, 0:1], sq2[0:64, blkk * 512:(blkk + 1) * 512],
                              start=True, stop=True).then_inc(S["NMM"], 1)
                te.wait_ge(S["LV"], 16 * (u + 1))
                for ci in range(64):
                    vi = 64 * u + ci
                    if vi >= 1:
                        te.wait_ge(S["VHE"], vi)
                    te.transpose(psT[:, 0:64], vtt[0:64, ci * 128:(ci + 1) * 128],
                                 identt[0:64, 0:64]).then_inc(S["VTP"], 1)
                te.wait_ge(S["MH"], u + 1)
                for qb in range(NQB):
                    qi = u * NQB + qb
                    for bat in range(qb + 1):
                        gg = gbat(u, qb, bat)
                        te.wait_ge(S["EXP"], gg)
                        for j in range(4):
                            kt = 4 * bat + j
                            ins = te.matmul(psS[:, j * 512:(j + 1) * 512],
                                            khat[:, kt * 128:(kt + 1) * 128],
                                            qhat[:, qb * 512:(qb + 1) * 512],
                                            start=True, stop=True)
                            if j == 3:
                                ins.then_inc(S["SMM"], 1)
                        te.wait_ge(S["EXP"], gg + 1)
                        if bat == qb:
                            te.wait_ge(S["MSK"], u * NQB + qb + 1)
                        if bat == 0:
                            te.wait_ge(S["AOB"], qi)
                        for j in range(4):
                            kt = 4 * bat + j
                            ins = te.matmul(psO[0:65, 0:512], vh[:, kt * 65:(kt + 1) * 65],
                                            atb[:, j * 512:(j + 1) * 512],
                                            start=(bat == 0 and j == 0),
                                            stop=(bat == qb and j == 3))
                            if j == 3:
                                ins.then_inc(S["VMM"], 1)
                    te.wait_ge(S["RCP"], qi + 1)
                    if qi >= 1:
                        te.wait_ge(S["BEV"], qi)
                    te.matmul(psB[0:64, 0:512], onesr[0:1, 0:64], rr[0:1, :],
                              start=True, stop=True).then_inc(S["BMM"], 1)
    return nc



def build_l1ffn():
    nc = bass.Bass("TRN2", target_bir_lowering=False)
    ao_d = nc.dram_tensor("aoT_c", [D, TPC], BF16, kind="ExternalInput").ap()
    x2_d = nc.dram_tensor("x2", [TPC, D], F32, kind="ExternalInput").ap()
    wo_d = nc.dram_tensor("woT", [D, D], F32, kind="ExternalInput").ap()
    wr_d = nc.dram_tensor("wr_t", [D, 8], F32, kind="ExternalInput").ap()
    br_d = nc.dram_tensor("br", [1, 8], F32, kind="ExternalInput").ap()
    w1_d = nc.dram_tensor("w1t", [8, D, 4096], F32, kind="ExternalInput").ap()
    b1_d = nc.dram_tensor("b1_t", [128, 8 * 32], F32, kind="ExternalInput").ap()
    w2_d = nc.dram_tensor("w2t", [8, 4096, D], F32, kind="ExternalInput").ap()
    b2_d = nc.dram_tensor("b2", [8, D], F32, kind="ExternalInput").ap()
    gm_d = nc.dram_tensor("gamma", [1, D], F32, kind="ExternalInput").ap()
    bt_d = nc.dram_tensor("beta", [1, D], F32, kind="ExternalInput").ap()
    ident = nc.dram_tensor("ident", [128, 128], BF16, kind="ExternalInput").ap()
    o_out = nc.dram_tensor("o_out", [TPC, D], F32, kind="ExternalOutput").ap()
    x3_h = nc.dram_tensor("x3_h", [TPC, D], F32, kind="Internal").ap()

    NTT = TPC // 128
    NQ = 8
    HKC = 4
    NI = 8 * NQ
    with ExitStack() as ctx:
        sb = lambda name, shape, dt=F32: ctx.enter_context(nc.sbuf_tensor(name, shape, dt))
        identt = sb("identt", [128, 128], BF16)
        rwb = sb("rwb", [128, 8 * 8], BF16)
        brrow = sb("brrow", [1, 8], BF16)
        b1t = sb("b1t", [128, 8 * 32])
        b2b = sb("b2b", [8, D], BF16)
        onesb = sb("onesb", [1, 128], BF16)
        gmrow = sb("gmrow", [1, D], BF16)
        btrow = sb("btrow", [1, D], BF16)
        grep = sb("grep", [128, D], BF16)
        brep2 = sb("brep2", [128, D], BF16)
        aostg = [sb(f"aostg{p}", [128, 8 * 128], BF16) for p in range(2)]
        wosb = sb("wosb", [128, 8 * 1024], BF16)
        x3T = [sb(f"x3T{k}", [128, TPC], BF16) for k in range(8)]
        xtile = [sb(f"xtile{p}", [128, D]) for p in range(2)]
        xbf = [sb(f"xbf{p}", [128, D], BF16) for p in range(2)]
        gcol = sb("gcol", [128, NTT * 8])
        gbf = sb("gbf", [128, 8], BF16)
        gtT = sb("gtT", [8, TPC], BF16)
        esb = sb("esb", [128, 8])
        mcol = sb("mcol", [128, 1])
        e2c = sb("e2c", [128, 1])
        rden = sb("rden", [128, 1])
        msk = sb("msk", [128, 8])
        indb = sb("indb", [128, 8])
        fla = sb("fla", [128, 256])
        flb = sb("flb", [128, 256])
        w1sb = [sb(f"w1sb{p}", [128, 4 * 1024], BF16) for p in range(2)]
        w2sb = [sb("w2sb0", [128, 4 * 1024], BF16)]
        hidq = [sb(f"hidq{kc}", [128, TPC], BF16) for kc in range(HKC)]
        acc = [sb(f"acc{t}", [128, D]) for t in range(NTT)]
        t3 = [sb(f"t3{p}", [128, D]) for p in range(2)]
        mu = sb("mu", [128, 1])
        var = sb("var", [128, 1])
        vsc = sb("vsc", [128, 1])
        vsq = sb("vsq", [128, 1])
        rstd = sb("rstd", [128, 1])
        sqscr = sb("sqscr", [128, D])
        inn = sqscr
        outt = [sb(f"outt{p}", [128, D]) for p in range(2)]
        x3t = outt

        psg = ctx.enter_context(nc.psum_tensor("psg", [8, 128], BF16))
        psT = [ctx.enter_context(nc.psum_tensor(f"psT{i}", [128, 128], BF16)) for i in range(2)]
        psh = [ctx.enter_context(nc.psum_tensor(f"psh{i}", [128, 512], F32)) for i in range(2)]
        pso = [ctx.enter_context(nc.psum_tensor("pso0", [128, 1024], F32))]
        psr = ctx.enter_context(nc.psum_tensor("psr", [128, 8], F32))

        S = _mk(ctx, nc, ["LW", "LAO0", "LAO1", "LX0", "LX1", "WOM", "X3R", "SQ1",
                          "V1", "XB", "TP", "TE", "RMM", "RE", "GT", "GTT",
                          "GTE", "GD", "W1L0", "W1L1", "W2L0", "W2L1", "HMM",
                          "HEV", "OMM", "ACC", "BMM", "LR0", "LR1", "SUM",
                          "CTR", "VAR", "LNF", "STO0", "STO1", "STX0", "STX1",
                          "BCG"])
        blk = ctx.enter_context(nc.Block())

        @blk.sync
        def _(sync):
            for tt in range(NTT):
                p = tt % 2
                if tt >= 2:
                    sync.wait_ge(S["X3R"], tt - 1)
                    sync.wait_ge(S["WOM"], 16 * (tt - 1))
                for k in range(8):
                    sync.dma_start(out=aostg[p][:, k * 128:(k + 1) * 128],
                                   in_=ao_d[k * 128:(k + 1) * 128, tt * 128:(tt + 1) * 128]
                                   ).then_inc(S[f"LAO{p}"], 16)
                sync.dma_start(out=xtile[p][:], in_=x2_d[tt * 128:(tt + 1) * 128, :]).then_inc(S[f"LX{p}"], 16)
            sync.wait_ge(S["GD"], NTT)
            for tt in range(NTT):
                p = tt % 2
                if tt >= 2:
                    sync.wait_ge(S["SUM"], tt - 1)
                sync.wait_ge(S[f"STX{p}"], 16 * (tt // 2 + 1))
                sync.dma_start(out=xtile[p][:], in_=x3_h[tt * 128:(tt + 1) * 128, :]).then_inc(S[f"LR{p}"], 16)

        @blk.gpsimd
        def _(g):
            g.dma_start(out=identt[:], in_=ident[:, :]).then_inc(S["LW"], 16)
            for k in range(8):
                g.dma_start(out=rwb[:, k * 8:(k + 1) * 8],
                            in_=wr_d[k * 128:(k + 1) * 128, :]).then_inc(S["LW"], 16)
            g.dma_start(out=brrow[:], in_=br_d[:, :]).then_inc(S["LW"], 16)
            g.dma_start(out=b1t[:], in_=b1_d[:, :]).then_inc(S["LW"], 16)
            g.dma_start(out=b2b[:], in_=b2_d[:, :]).then_inc(S["LW"], 16)
            g.dma_start(out=gmrow[:], in_=gm_d[:, :]).then_inc(S["LW"], 16)
            g.dma_start(out=btrow[:], in_=bt_d[:, :]).then_inc(S["LW"], 16)
            for k in range(8):
                g.dma_start(out=wosb[:, k * 1024:(k + 1) * 1024],
                            in_=wo_d[k * 128:(k + 1) * 128, :]).then_inc(S["LW"], 16)
            for tt in range(NTT):
                g.wait_ge(S["X3R"], tt + 1)
                g.dma_start(out=x3_h[tt * 128:(tt + 1) * 128, :],
                            in_=x3t[tt % 2][:]).then_inc(S[f"STX{tt % 2}"], 16)

            def casts(i):
                e, q = i // NQ, i % NQ
                p = i % 2
                if i >= 2:
                    g.wait_ge(S["HMM"], 128 * (i - 1))
                for kc in range(4):
                    c0 = q * 512 + kc * 128
                    g.dma_start(out=w1sb[p][:, kc * 1024:(kc + 1) * 1024],
                                in_=w1_d[e, :, c0:c0 + 128]
                                .rearrange("(k pp) c -> pp k c", pp=128)
                                ).then_inc(S[f"W1L{p}"], 16)
                if i >= 1:
                    g.wait_ge(S["OMM"], 128 * i)
                for kc in range(4):
                    g.dma_start(out=w2sb[0][:, kc * 1024:(kc + 1) * 1024],
                                in_=w2_d[e, q * 512 + kc * 128:q * 512 + (kc + 1) * 128, :]
                                ).then_inc(S["W2L0"], 16)

            casts(0)
            casts(1)
            for i in range(NI):
                if i + 2 < NI:
                    casts(i + 2)
            for tt in range(NTT):
                g.wait_ge(S["LNF"], tt + 1)
                g.dma_start(out=o_out[tt * 128:(tt + 1) * 128, :],
                            in_=outt[tt % 2][:]).then_inc(S[f"STO{tt % 2}"], 16)
            g.wait_ge(S["STO0"], 16 * 8)
            g.wait_ge(S["STO1"], 16 * 8)

        @blk.vector
        def _(v):
            def fl():
                v.tensor_copy(flb[:], fla[:])

            for t in range(NTT):
                v.memset(acc[t][:], 0.0)
            v.memset(onesb[:], 1.0).then_inc(S["BCG"], 1)
            def front_tile(tt):
                p = tt % 2
                v.wait_ge(S["WOM"], 16 * (tt + 1))
                v.wait_ge(S[f"LX{p}"], 16 * (tt // 2 + 1))
                if tt >= 2:
                    v.wait_ge(S[f"STX{p}"], 16 * (tt // 2))
                    v.wait_ge(S["TP"], 8 * (tt - 1))
                v.tensor_tensor(t3[p][:, 0:512], pso[0][:, 0:512], xtile[p][:, 0:512], ALU.add)
                v.tensor_tensor(t3[p][:, 512:1024], pso[0][:, 512:1024],
                                xtile[p][:, 512:1024], ALU.add).then_inc(S["V1"], 1)
                v.tensor_reduce(mu[:], t3[p][:], mybir.AxisListType.X, ALU.add)
                fl()
                v.tensor_scalar_mul(mu[:], mu[:], 1.0 / D)
                fl()
                v.tensor_scalar(t3[p][:], t3[p][:], mu[:, 0:1], None,
                                ALU.subtract).then_inc(S["CTR"], 1)
                v.wait_ge(S["VAR"], 2 * tt + 1)
                v.tensor_scalar(vsc[:], var[:], 1.0 / D, 1e-5, ALU.mult,
                                ALU.add).then_inc(S["CTR"], 1)
                v.wait_ge(S["VAR"], 2 * tt + 2)
                v.reciprocal(rstd[:], vsq[:])
                fl()
                v.tensor_scalar_mul(x3t[p][:], t3[p][:], rstd[:, 0:1]).then_inc(S["X3R"], 1)
                v.tensor_copy(xbf[p][:], x3t[p][:]).then_inc(S["XB"], 1)

            def router_tile(tt):
                v.wait_ge(S["RMM"], tt + 1)
                v.reduce_max(mcol[:], psr[:, 0:8], axis=mybir.AxisListType.X,
                             negate=True).then_inc(S["GT"], 1)
                v.wait_ge(S["RE"], tt + 1)
                v.tensor_single_scalar(msk[:], esb[:], 1.0, ALU.is_ge)
                fl()
                v.scalar_tensor_tensor(indb[:], msk[:], -2.0, esb[:], ALU.mult, ALU.add)
                fl()
                v.reduce_max(e2c[:], indb[:], axis=mybir.AxisListType.X)
                fl()
                v.tensor_scalar(rden[:], e2c[:], 1.0, None, ALU.add)
                fl()
                v.reciprocal(rden[:], rden[:])
                fl()
                v.tensor_single_scalar(indb[:], esb[:], e2c[:, 0:1], ALU.is_ge)
                fl()
                v.scalar_tensor_tensor(gcol[:, tt * 8:(tt + 1) * 8], esb[:], rden[:, 0:1],
                                       indb[:], ALU.mult, ALU.mult)
                if tt >= 1:
                    v.wait_ge(S["GTT"], tt)
                fl()
                v.tensor_copy(gbf[:], gcol[:, tt * 8:(tt + 1) * 8]).then_inc(S["GD"], 1)

            for it in range(NTT):
                front_tile(it)
            for it in range(NTT):
                router_tile(it)
            for i in range(NI):
                e = i // NQ
                for tt in range(NTT):
                    v.wait_ge(S["OMM"], 128 * i + 8 * (tt + 1))
                    v.scalar_tensor_tensor(acc[tt][:], pso[0][:],
                                           gcol[:, tt * 8 + e:tt * 8 + e + 1],
                                           acc[tt][:], ALU.mult, ALU.add).then_inc(S["ACC"], 1)
            for tt in range(NTT):
                p = tt % 2
                v.wait_ge(S["BMM"], tt + 1)
                v.wait_ge(S[f"LR{p}"], 16 * (tt // 2 + 1))
                v.wait_ge(S["BCG"], 3)
                if tt < 2:
                    v.wait_ge(S["STX0"], 16 * 8)
                    v.wait_ge(S["STX1"], 16 * 8)
                v.tensor_tensor(t3[p][:, 0:512], acc[tt][:, 0:512], psh[0][:], ALU.add)
                v.tensor_tensor(t3[p][:, 512:1024], acc[tt][:, 512:1024], psh[1][:], ALU.add)
                v.tensor_tensor(t3[p][:], t3[p][:], xtile[p][:], ALU.add)
                v.tensor_reduce(mu[:], t3[p][:], mybir.AxisListType.X, ALU.add)
                fl()
                v.tensor_scalar_mul(mu[:], mu[:], 1.0 / D)
                fl()
                v.tensor_scalar(t3[p][:], t3[p][:], mu[:, 0:1], None,
                                ALU.subtract).then_inc(S["CTR"], 1)
                v.wait_ge(S["VAR"], 2 * NTT + 4 * tt + 1)
                v.tensor_scalar(vsc[:], var[:], 1.0 / D, 1e-5, ALU.mult,
                                ALU.add).then_inc(S["CTR"], 1)
                v.wait_ge(S["VAR"], 2 * NTT + 4 * tt + 2)
                v.reciprocal(rstd[:], vsq[:])
                fl()
                v.tensor_scalar_mul(inn[:], t3[p][:], rstd[:, 0:1])
                v.tensor_tensor(inn[:], inn[:], grep[:], ALU.mult)
                v.tensor_tensor(inn[:], inn[:], brep2[:], ALU.add)
                v.tensor_tensor(t3[p][:], inn[:], xtile[p][:], ALU.add).then_inc(S["SUM"], 1)
                v.tensor_reduce(mu[:], t3[p][:], mybir.AxisListType.X, ALU.add)
                fl()
                v.tensor_scalar_mul(mu[:], mu[:], 1.0 / D)
                fl()
                v.tensor_scalar(t3[p][:], t3[p][:], mu[:, 0:1], None,
                                ALU.subtract).then_inc(S["CTR"], 1)
                v.wait_ge(S["VAR"], 2 * NTT + 4 * tt + 3)
                v.tensor_scalar(vsc[:], var[:], 1.0 / D, 1e-5, ALU.mult,
                                ALU.add).then_inc(S["CTR"], 1)
                v.wait_ge(S["VAR"], 2 * NTT + 4 * tt + 4)
                v.reciprocal(rstd[:], vsq[:])
                if tt >= 2:
                    v.wait_ge(S[f"STO{p}"], 16 * (tt // 2))
                fl()
                v.tensor_scalar_mul(outt[p][:], t3[p][:], rstd[:, 0:1]).then_inc(S["LNF"], 1)

        @blk.scalar
        def _(act):
            act.wait_ge(S["BCG"], 2)
            act.copy(grep[:, 0:512], psh[0][:])
            act.copy(grep[:, 512:1024], psh[1][:])
            act.copy(brep2[:, 0:512], pso[0][:, 0:512])
            act.copy(brep2[:, 512:1024], pso[0][:, 512:1024]).then_inc(S["BCG"], 1)
            for tt in range(NTT):
                act.wait_ge(S["CTR"], 2 * tt + 1)
                act.activation(out=sqscr[:], in_=t3[tt % 2][:], func=AF.Square,
                               accum_out=var[:, 0:1]).then_inc(S["VAR"], 1)
                act.wait_ge(S["CTR"], 2 * tt + 2)
                act.activation(out=vsq[:], in_=vsc[:], func=AF.Sqrt).then_inc(S["VAR"], 1)
                for t in range(8):
                    gidx = tt * 8 + t
                    act.wait_ge(S["TP"], gidx + 1)
                    act.copy(x3T[t][:, tt * 128:(tt + 1) * 128], psT[gidx % 2][:]).then_inc(S["TE"], 1)
            for tt in range(NTT):
                act.wait_ge(S["GT"], tt + 1)
                act.activation(out=esb[:], in_=psr[:, 0:8], func=AF.Exp,
                               bias=mcol[:, 0:1]).then_inc(S["RE"], 1)
                act.wait_ge(S["GTT"], tt + 1)
                act.copy(gtT[0:8, tt * 128:(tt + 1) * 128], psg[0:8, :]).then_inc(S["GTE"], 1)
            for i in range(NI):
                e, q = i // NQ, i % NQ
                for kc in range(HKC):
                    for tb in range(4):
                        j = i * 16 + kc * 4 + tb
                        bcol = b1t[:, e * 32 + q * 4 + kc:e * 32 + q * 4 + kc + 1]
                        act.wait_ge(S["HMM"], 8 * (j + 1))
                        act.activation(out=hidq[kc][:, tb * 512:(tb + 1) * 512],
                                       in_=psh[j % 2][:], func=AF.Relu,
                                       bias=bcol).then_inc(S["HEV"], 1)
            for tt in range(NTT):
                for r in range(4):
                    cidx = 2 * NTT + 4 * tt + r + 1
                    act.wait_ge(S["CTR"], cidx)
                    if r % 2 == 0:
                        act.activation(out=sqscr[:], in_=t3[tt % 2][:], func=AF.Square,
                                       accum_out=var[:, 0:1]).then_inc(S["VAR"], 1)
                    else:
                        act.activation(out=vsq[:], in_=vsc[:], func=AF.Sqrt).then_inc(S["VAR"], 1)

        @blk.tensor
        def _(te):
            te.wait_ge(S["LW"], 16 * 22)
            te.wait_ge(S["BCG"], 1)
            te.matmul(psh[0][:], onesb[0:1, :], gmrow[0:1, 0:512], start=True, stop=True)
            te.matmul(psh[1][:], onesb[0:1, :], gmrow[0:1, 512:1024], start=True, stop=True)
            te.matmul(pso[0][:, 0:512], onesb[0:1, :], btrow[0:1, 0:512], start=True, stop=True)
            te.matmul(pso[0][:, 512:1024], onesb[0:1, :], btrow[0:1, 512:1024],
                      start=True, stop=True).then_inc(S["BCG"], 1)
            te.wait_ge(S["BCG"], 3)
            for it in range(NTT + 2):
                if it < NTT:
                    tt = it
                    p = tt % 2
                    te.wait_ge(S[f"LAO{p}"], 128 * (tt // 2 + 1))
                    if tt >= 1:
                        te.wait_ge(S["V1"], tt)
                    for k in range(8):
                        for h in range(2):
                            ins = te.matmul(pso[0][:, h * 512:(h + 1) * 512],
                                            aostg[p][:, k * 128:(k + 1) * 128],
                                            wosb[:, k * 1024 + h * 512:k * 1024 + (h + 1) * 512],
                                            start=(k == 0), stop=(k == 7))
                            if k == 7 and h == 1:
                                ins.then_inc(S["WOM"], 16)
                if it >= 2:
                    tt = it - 2
                    te.wait_ge(S["XB"], tt + 1)
                    for t in range(8):
                        gidx = tt * 8 + t
                        if gidx >= 2:
                            te.wait_ge(S["TE"], gidx - 1)
                        te.transpose(psT[gidx % 2][:], xbf[tt % 2][:, t * 128:(t + 1) * 128],
                                     identt[:]).then_inc(S["TP"], 1)
            for tt in range(NTT):
                te.wait_ge(S["TE"], 8 * NTT)
                if tt >= 1:
                    te.wait_ge(S["RE"], tt)
                for k in range(8):
                    te.matmul(psr[:, 0:8], x3T[k][:, tt * 128:(tt + 1) * 128],
                              rwb[:, k * 8:(k + 1) * 8], start=(k == 0), stop=False)
                te.matmul(psr[:, 0:8], onesb[0:1, :], brrow[0:1, 0:8],
                          start=False, stop=True).then_inc(S["RMM"], 1)
                te.wait_ge(S["GD"], tt + 1)
                if tt >= 1:
                    te.wait_ge(S["GTE"], tt)
                te.transpose(psg[:, :], gbf[:, 0:8], identt[:]).then_inc(S["GTT"], 1)
            for i in range(NI):
                e, q = i // NQ, i % NQ
                p = i % 2
                te.wait_ge(S[f"W1L{p}"], 64 * (i // 2 + 1))
                te.wait_ge(S["TE"], NTT * 8)
                for kc in range(HKC):
                    for tb in range(4):
                        j = i * 16 + kc * 4 + tb
                        if j >= 2:
                            te.wait_ge(S["HEV"], j - 1)
                        for k in range(8):
                            ins = te.matmul(psh[j % 2][:],
                                            w1sb[p][:, kc * 1024 + k * 128:kc * 1024 + (k + 1) * 128],
                                            x3T[k][:, tb * 512:(tb + 1) * 512],
                                            start=(k == 0), stop=(k == 7))
                            if k == 7:
                                ins.then_inc(S["HMM"], 8)
                te.wait_ge(S["W2L0"], 64 * (i + 1))
                te.wait_ge(S["HEV"], 16 * (i + 1))
                for tt in range(NTT):
                    gidx = NTT * i + tt
                    if gidx >= 1:
                        te.wait_ge(S["ACC"], gidx)
                    for kc in range(HKC):
                        for h in range(2):
                            ins = te.matmul(pso[0][:, h * 512:(h + 1) * 512],
                                            hidq[kc][:, tt * 128:(tt + 1) * 128],
                                            w2sb[0][:, kc * 1024 + h * 512:kc * 1024 + (h + 1) * 512],
                                            start=(kc == 0), stop=(kc == 3))
                            if kc == 3 and h == 1:
                                ins.then_inc(S["OMM"], 8)
            te.wait_ge(S["GTE"], NTT)
            te.wait_ge(S["HEV"], 16 * NI)
            for tt in range(NTT):
                if tt >= 1:
                    te.wait_ge(S["SUM"], tt)
                for h in range(2):
                    ins = te.matmul(psh[h][:], gtT[0:8, tt * 128:(tt + 1) * 128],
                                    b2b[0:8, h * 512:(h + 1) * 512], start=True, stop=True)
                    if h == 1:
                        ins.then_inc(S["BMM"], 1)
    return nc



def _c(a):
    return np.ascontiguousarray(a)


def kernel(**inputs):
    inp = {k: np.asarray(v) for k, v in inputs.items()}
    x = inp["L0_attn_x"]
    ident = np.eye(128, dtype=BFD)
    cores = list(range(NCORES))

    def tok(c):
        b, q = c // 4, c % 4
        return b, q, q * 2048

    convw_t = _c(inp["L0_attn_conv_w"].reshape(16, 128, 4).transpose(1, 0, 2).reshape(128, 64))
    convb_t = _c(inp["L0_attn_conv_b"].reshape(16, 128).T)
    dtb_t = _c(inp["L0_attn_dt_proj_b"].reshape(16, 128).T)
    mapsA = []
    for c in cores:
        b, q, t0 = tok(c)
        halo = x[b, t0 - HALO:t0, :] if q > 0 else np.zeros((HALO, D), np.float32)
        mapsA.append({
            "xh": _c(np.concatenate([halo, x[b, t0:t0 + TPC, :]], 0)),
            "w_in": inp["L0_attn_in_proj_w"], "n1w": inp["L0_attn_norm1_w"][None, :],
            "convw_t": convw_t, "convb_t": convb_t, "xpw": inp["L0_attn_x_proj_w"],
            "dtw": inp["L0_attn_dt_proj_w"], "dtb_t": dtb_t, "ident": ident})
    rA = run_bass_kernel_spmd(build_l0front(), mapsA, core_ids=cores).results

    xc_full = np.empty((D_INNER, B * L), np.float32)
    dt_full = np.empty((D_INNER, B * L), np.float32)
    Bf = np.empty((B, 16, L), np.float32)
    Cf = np.empty((B, 16, L), np.float32)
    for c in cores:
        b, q, t0 = tok(c)
        cols = slice(b * L + t0, b * L + t0 + TPC)
        xc_full[:, cols] = rA[c]["o_xc"]
        dt_full[:, cols] = rA[c]["o_dt"]
        Bf[b, :, t0:t0 + TPC] = rA[c]["o_BC"][0:16]
        Cf[b, :, t0:t0 + TPC] = rA[c]["o_BC"][16:32]

    mapsB = []
    for c in cores:
        ch0 = c * 256
        mapsB.append({"xc_r": _c(xc_full[ch0:ch0 + 256]), "dt_r": _c(dt_full[ch0:ch0 + 256]),
                      "Bf": Bf, "Cf": Cf,
                      "alog_r": _c(inp["L0_attn_A_log"][ch0:ch0 + 256])})
    rB = run_bass_kernel_spmd(build_scan(), mapsB, core_ids=cores).results
    ys_full = np.concatenate([rB[c]["o_ys"] for c in cores], 0)

    D_t = _c(inp["L0_attn_D_param"].reshape(16, 128).T)
    mapsC = []
    for c in cores:
        b, q, t0 = tok(c)
        cols = slice(b * L + t0, b * L + t0 + TPC)
        mapsC.append({"ys_c": _c(ys_full[:, cols]), "xc_c": rA[c]["o_xc"],
                      "z_c": rA[c]["o_z"], "xorig": _c(x[b, t0:t0 + TPC]),
                      "D_t": D_t, "opw": inp["L0_attn_out_proj_w"],
                      "n2w": inp["L0_attn_norm2_w"][None, :], "ident": ident})
    rC = run_bass_kernel_spmd(build_l0back(), mapsC, core_ids=cores).results

    b1m_t = _c(inp["L0_attn_b1"].reshape(8, 16, 128).transpose(2, 0, 1).reshape(128, 128))
    mapsD = []
    for c in cores:
        b, q, t0 = tok(c)
        mapsD.append({"htT": rC[c]["o_htT"], "xin1": rC[c]["o_xin1"],
                      "xorig": _c(x[b, t0:t0 + TPC]), "rw": inp["L0_attn_router_w"],
                      "w1": inp["L0_attn_w1"], "b1_t": b1m_t, "w2": inp["L0_attn_w2"],
                      "b2": inp["L0_attn_b2"], "ident": ident})
    rD = run_bass_kernel_spmd(build_l0moe(), mapsD, core_ids=cores).results

    b1f_t = _c(inp["L0_ffn_b1"].reshape(8, 32, 128).transpose(2, 0, 1).reshape(128, 256))
    mapsD2 = []
    for c in cores:
        mapsD2.append({"x1": rD[c]["o_x1"], "w1": inp["L0_ffn_W1"], "b1_t": b1f_t,
                       "w2": inp["L0_ffn_W2"], "b2": inp["L0_ffn_b2"], "ident": ident})
    rD2 = run_bass_kernel_spmd(build_l0ffn(), mapsD2, core_ids=cores).results

    mapsE = []
    for c in cores:
        mapsE.append({"x2": rD2[c]["o_x2"], "wqT": _c(inp["L1_attn_Wq"].T),
                      "wkT": _c(inp["L1_attn_Wk"].T), "wvT": _c(inp["L1_attn_Wv"].T),
                      "ident": ident})
    rE = run_bass_kernel_spmd(build_qkv(), mapsE, core_ids=cores).results

    qT = np.empty((B, D, L), BFD); kT = np.empty((B, D, L), BFD); vT = np.empty((B, D, L), BFD)
    for c in cores:
        b, q, t0 = tok(c)
        qT[b][:, t0:t0 + TPC] = rE[c]["o_qT"]
        kT[b][:, t0:t0 + TPC] = rE[c]["o_kT"]
        vT[b][:, t0:t0 + TPC] = rE[c]["o_vT"]

    mapsF = []
    for c in cores:
        qh = np.empty((4, 64, L), BFD); kh = np.empty((4, 64, L), BFD); vh = np.empty((4, 64, L), BFD)
        for i in range(4):
            u = 4 * c + i
            b, h = u // 16, u % 16
            qh[i] = qT[b][h * 64:(h + 1) * 64, :]
            kh[i] = kT[b][h * 64:(h + 1) * 64, :]
            vh[i] = vT[b][h * 64:(h + 1) * 64, :]
        mapsF.append({"qT_h": qh, "kT_h": kh, "vT_h": vh, "ident": ident})
    rF = run_bass_kernel_spmd(build_attn(), mapsF, core_ids=cores).results

    aoT = np.empty((B, D, L), BFD)
    for c in cores:
        for i in range(4):
            u = 4 * c + i
            b, h = u // 16, u % 16
            aoT[b][h * 64:(h + 1) * 64, :] = rF[c]["o_aoT"][i]

    b1l_t = _c(inp["L1_ffn_b1"].reshape(8, 32, 128).transpose(2, 0, 1).reshape(128, 256))
    w1t = _c(inp["L1_ffn_W1"].transpose(0, 2, 1))
    w2t = _c(inp["L1_ffn_W2"].transpose(0, 2, 1))
    mapsG = []
    for c in cores:
        b, q, t0 = tok(c)
        mapsG.append({"aoT_c": _c(aoT[b][:, t0:t0 + TPC]), "x2": rD2[c]["o_x2"],
                      "woT": _c(inp["L1_attn_Wo"].T), "wr_t": _c(inp["L1_ffn_Wr"].T),
                      "br": inp["L1_ffn_br"][None, :], "w1t": w1t, "b1_t": b1l_t,
                      "w2t": w2t, "b2": inp["L1_ffn_b2"],
                      "gamma": inp["L1_ffn_gamma"][None, :],
                      "beta": inp["L1_ffn_beta"][None, :], "ident": ident})
    rG = run_bass_kernel_spmd(build_l1ffn(), mapsG, core_ids=cores).results

    out = np.empty((B, L, D), np.float32)
    for c in cores:
        b, q, t0 = tok(c)
        out[b, t0:t0 + TPC] = rG[c]["o_out"]
    return out
